# version 05 (final)
# baseline (speedup 1.0000x reference)
import numpy as np
from contextlib import ExitStack
from collections import defaultdict

import concourse.bass as bass
import concourse.mybir as mybir

F32 = mybir.dt.float32
BF16 = mybir.dt.bfloat16
AF = mybir.ActivationFunctionType
ALU = mybir.AluOpType

NEG = -1e30


class Em:
    ENGS = ("t", "v", "a", "p", "s")
    NDMA = 8

    def __init__(self, nc, es):
        self.nc = nc
        self.cnt = defaultdict(int)
        self.waited = {e: defaultdict(int) for e in self.ENGS}
        self.ops = {e: [] for e in self.ENGS}
        self.sems = {}
        for name in ("t", "v", "a", "p", "s"):
            self.sems[name] = es.enter_context(nc.semaphore("sem_" + name))
        for e in self.ENGS:
            for k in range(self.NDMA):
                n = f"d{e}{k}"
                self.sems[n] = es.enter_context(nc.semaphore("sem_" + n))
        self.dma_rot = {e: 0 for e in self.ENGS}
        self.n_cc = 0
        self.es = es

    def _w(self, e, waits):
        out = []
        for tk in waits:
            if tk is None:
                continue
            (sn, val) = tk
            if val <= self.waited[e][sn]:
                continue
            self.waited[e][sn] = val
            out.append((sn, val))
        return out

    def op(self, e, fn, waits=()):
        ws = self._w(e, list(waits))
        self.cnt[e] += 1
        tok = (e, self.cnt[e])
        sems = self.sems

        def run(eng):
            for (sn, val) in ws:
                eng.wait_ge(sems[sn], val)
            fn(eng).then_inc(sems[e], 1)
        self.ops[e].append(run)
        return tok

    def dma(self, e, fn, waits=()):
        k = self.dma_rot[e]
        self.dma_rot[e] = (k + 1) % self.NDMA
        dn = f"d{e}{k}"
        prev = self.cnt[dn]
        waits = list(waits)
        if prev:
            waits.append((dn, prev))
        ws = self._w(e, waits)
        self.cnt[dn] += 16
        tok = (dn, self.cnt[dn])
        sems = self.sems

        def run(eng):
            for (sn, val) in ws:
                eng.wait_ge(sems[sn], val)
            fn(eng).then_inc(sems[dn], 16)
        self.ops[e].append(run)
        return tok

    def cc(self, fn, waits=()):
        sem = self.es.enter_context(self.nc.semaphore(f"sem_cc{self.n_cc}"))
        key = f"cc{self.n_cc}"
        self.n_cc += 1
        self.sems[key] = sem
        ws = self._w("p", list(waits))
        tok = (key, 1)
        sems = self.sems

        def run(eng):
            for (sn, val) in ws:
                eng.wait_ge(sems[sn], val)
            fn(eng).then_inc(sem, 1)
        self.ops["p"].append(run)
        return tok

    def barrier(self):
        targets = []
        for e in self.ENGS:
            if self.cnt[e]:
                targets.append((e, self.cnt[e]))
            for k in range(self.NDMA):
                dn = f"d{e}{k}"
                if self.cnt[dn]:
                    targets.append((dn, self.cnt[dn]))
        for i in range(self.n_cc):
            targets.append((f"cc{i}", 1))
        toks = []
        for e in self.ENGS:
            toks.append(self.op(e, lambda eng: eng.nop(), waits=targets))
        final = []
        for e in self.ENGS:
            final.append(self.op(e, lambda eng: eng.nop(), waits=toks))
        return final

    def finalize(self, block):
        for e, attr in (("t", "tensor"), ("v", "vector"), ("a", "scalar"),
                        ("p", "gpsimd"), ("s", "sync")):
            ops = self.ops[e]
            if not ops:
                continue

            def body(eng, ops=ops):
                for r in ops:
                    r(eng)
            getattr(block, attr)(body)


def tl(*toks):
    out = []
    for t in toks:
        if t is None:
            continue
        if isinstance(t, list):
            out.extend(x for x in t if x is not None)
        else:
            out.append(t)
    return out


class Cfg:
    def __init__(self, Lc=2048, TM=256, E=8, HID=2048, cap=2048):
        self.Lc, self.TM, self.E, self.HID, self.cap = Lc, TM, E, HID, cap
        self.D, self.DI, self.DS, self.DTR = 1024, 2048, 16, 64
        self.HQ, self.HKV, self.HD = 16, 4, 64
        self.DC = self.D // 128
        self.DIC = self.DI // 128
        self.NMC = Lc // TM
        self.NT = Lc // 128
        self.KB = min(512, Lc)
        self.NKB = Lc // self.KB
        self.KBQ = self.KB // 128
        self.HC = HID // 128
        self.TE = min(512, Lc)
        self.NTE = Lc // self.TE
        self.TH = max(Lc // 2, self.TE)
        self.NTH = Lc // self.TH


def build(cfg, stages=("ma", "f0", "mb", "f1")):
    nc = bass.Bass("TRN2", target_bir_lowering=False)
    c = cfg
    Lc, D, DI, DS, DTR, TM = c.Lc, c.D, c.DI, c.DS, c.DTR, c.TM
    DC, DIC, NMC, NT = c.DC, c.DIC, c.NMC, c.NT
    HQ, HKV, HD, E, HID = c.HQ, c.HKV, c.HD, c.E, c.HID

    def din(name, shape, dt=F32):
        return nc.dram_tensor(name, list(shape), dt, kind="ExternalInput").ap()

    x_in = din("x", [Lc, D])
    xp3a = din("xp3a", [128, DC * 3])
    W = {}
    for pre in ("a", "b"):
        W[pre + "Win"] = din(pre + "Win", [D, 2 * DI])
        W[pre + "convw"] = din(pre + "convw", [128, DIC * 4])
        W[pre + "convb"] = din(pre + "convb", [128, DIC])
        W[pre + "Wx"] = din(pre + "Wx", [DI, 128])
        W[pre + "Alog"] = din(pre + "Alog", [128, DIC * DS])
        W[pre + "Dp"] = din(pre + "Dp", [128, DIC])
        W[pre + "Wout"] = din(pre + "Wout", [DI, D])
    W["aWdt"] = din("aWdt", [DTR, DI])
    W["abdt"] = din("abdt", [128, DIC])
    for n_, sh in [("fg1", [128, DC]), ("fg2", [128, DC]), ("fwq", [D, HQ * HD]),
                   ("fwk", [D, HKV * HD]), ("fwv", [D, HKV * HD]),
                   ("fwo", [HQ * HD, D]), ("frw", [D, E]), ("frb", [128, E]),
                   ("fw1", [E, D, HID]), ("fw2", [E, D, HID]), ("fw3", [E, HID, D])]:
        W[n_] = din(n_, sh)
    cos2 = din("cos2", [128, Lc], BF16)
    sin2 = din("sin2", [128, Lc], BF16)
    cmaskb = din("cmaskb", [128, c.KBQ * c.KB])
    cflag = din("cflag", [128, 8])
    csel = din("csel", [DS, DS * 128])
    csel16 = din("csel16", [NT, NT * 128])
    ctri = din("ctri", [128, 128])
    cioE = din("cioE", [128, E])
    cpred = din("cpred", [8, 1])
    cident = din("cident", [128, 128])
    ones1 = din("ones1", [1, 128])
    onesc = din("onesc", [128, 1])

    x_out = nc.dram_tensor("y", [Lc, D], F32, kind="ExternalOutput").ap()

    xdr = nc.dram_tensor("xdr", [DC, 128, Lc], F32).ap()
    hdr = nc.dram_tensor("hdr", [DC, 128, Lc], F32).ap()
    sp_xc = nc.dram_tensor("sp_xc", [NMC, DIC, 128, TM], F32).ap()
    sp_dt = nc.dram_tensor("sp_dt", [NMC, DIC, 128, TM], F32).ap()
    sp_z = nc.dram_tensor("sp_z", [NMC, DIC, 128, TM], F32).ap()
    sp_B = nc.dram_tensor("sp_B", [NMC, DS, TM], F32).ap()
    sp_C = nc.dram_tensor("sp_C", [NMC, DS, TM], F32).ap()
    sp_dr = nc.dram_tensor("sp_dr", [NMC, 1, TM], F32).ap()
    cc_h_in = nc.dram_tensor("cc_h_in", [128, DIC * DS], F32).ap()
    cc_h_out = nc.dram_tensor("cc_h_out", [256, DIC * DS], F32).ap()
    cc_x3_in = nc.dram_tensor("cc_x3_in", [DC * 128, 3], F32).ap()
    cc_x3_out = nc.dram_tensor("cc_x3_out", [2 * DC * 128, 3], F32).ap()
    ccki = [nc.dram_tensor(f"ccki{i}", [HKV * HD, Lc], BF16).ap() for i in range(2)]
    ccko = [nc.dram_tensor(f"ccko{i}", [2 * HKV * HD, Lc], BF16).ap() for i in range(2)]
    ccvi = [nc.dram_tensor(f"ccvi{i}", [Lc, HKV * HD], BF16).ap() for i in range(2)]
    ccvo = [nc.dram_tensor(f"ccvo{i}", [2 * Lc, HKV * HD], BF16).ap() for i in range(2)]
    cc_r_in = [nc.dram_tensor(f"cc_r_in{i}", [1, 2 * E], F32).ap() for i in range(2)]
    cc_r_out = [nc.dram_tensor(f"cc_r_out{i}", [8, 2 * E], F32).ap() for i in range(2)]

    PAIRS = [[0, 1], [2, 3], [4, 5], [6, 7]]
    ALL8 = [[0, 1, 2, 3, 4, 5, 6, 7]]

    es = ExitStack()
    with es:
        def sb(name, shape, dt=F32, st=es):
            return es.enter_context(nc.sbuf_tensor(name, shape, dt)) if st is es else st.enter_context(nc.sbuf_tensor(name, shape, dt))

        em = Em(nc, es)

        s_cflag = sb("s_cflag", [128, 8])
        s_csel = sb("s_csel", [DS, DS * 128])
        s_csel16 = sb("s_csel16", [NT, NT * 128])
        s_ctri = sb("s_ctri", [128, 128])
        s_cioE = sb("s_cioE", [128, E])
        s_cpred = sb("s_cpred", [8, 1])
        s_ident = sb("s_ident", [128, 128])
        s_ones1 = sb("s_ones1", [1, 128])
        s_onesc = sb("s_onesc", [128, 1])
        s_cmaskb = sb("s_cmaskb", [128, c.KBQ * c.KB])
        s_g1 = sb("s_g1", [128, DC])
        s_g2 = sb("s_g2", [128, DC])
        s_frb = sb("s_frb", [128, E])
        wstg = [sb(f"wstg{i}", [128, 2048]) for i in range(2)]
        wbfg = [sb(f"wbfg{i}", [128, 2048], BF16) for i in range(2)]
        stat = sb("stat", [1, 3 * Lc])
        srow = sb("srow", [1, 4 * (TM + 3)])
        sq_s = sb("sq_s", [128, max(TM + 3, 512)])
        t_s = sb("t_s", [128, max(TM + 3, 512)])
        u_s = sb("u_s", [128, TM])
        h_s = sb("h_s", [128, TM])
        small = sb("small", [128, 64])
        xtm = sb("xtm", [128, D])
        epsln = sb("epsln", [128, 1])
        epsrm = sb("epsrm", [128, 1])

        psS = es.enter_context(nc.psum_tensor("psS", [128, 2048], F32))
        psA = es.enter_context(nc.psum_tensor("psA", [128, 512], F32))
        psB = es.enter_context(nc.psum_tensor("psB", [128, 512], F32))
        psC = es.enter_context(nc.psum_tensor("psC", [128, 512], F32))
        psD = es.enter_context(nc.psum_tensor("psD", [128, 512], F32))

        tok_const = []
        tok_const.append(em.op("v", lambda e: e.memset(epsln[:], 1e-5)))
        tok_const.append(em.op("v", lambda e: e.memset(epsrm[:], 1e-6)))
        for ap_, tile in [(cflag, s_cflag), (csel, s_csel), (csel16, s_csel16),
                          (ctri, s_ctri), (cioE, s_cioE), (cpred, s_cpred),
                          (cident, s_ident), (ones1, s_ones1), (onesc, s_onesc),
                          (cmaskb, s_cmaskb),
                          (W["frb"], s_frb)]:
            tok_const.append(em.dma("s", lambda e, a=ap_, t=tile: e.dma_start(out=t[:], in_=a[:])))
        tok_const.append(em.dma("s", lambda e: e.dma_start(out=s_g1[:], in_=W["fg1"][:])))
        tok_const.append(em.dma("s", lambda e: e.dma_start(out=s_g2[:], in_=W["fg2"][:])))

        tok_pstr = {0: None, 1: None}
        t_ld_prev = None
        prev_ev = None
        for tt in range(NT):
            t_ld = em.dma("s", lambda e, tt=tt: e.dma_start(out=xtm[:], in_=x_in[tt * 128:(tt + 1) * 128, :]),
                          waits=tl(t_ld_prev))
            last = []
            for ci in range(DC):
                ps = [psA, psB][ci % 2]
                buf = ci % 2
                t_tr = em.op("t", lambda e, ci=ci, ps=ps: e.transpose(ps[:, 0:128], xtm[:, ci * 128:(ci + 1) * 128], s_ident[:]),
                             waits=tl([t_ld], tok_const, tok_pstr[ci % 2]))
                t_ev = em.op("v", lambda e, ps=ps, buf=buf: e.tensor_copy(out=t_s[:, buf * 128:(buf + 1) * 128], in_=ps[:, 0:128]),
                             waits=tl([t_tr], tok_pstr[ci % 2], [prev_ev] if prev_ev else None))
                prev_ev = t_ev
                t_st = em.dma("a", lambda e, ci=ci, tt=tt, buf=buf: e.dma_start(out=xdr[ci, :, tt * 128:(tt + 1) * 128], in_=t_s[:, buf * 128:(buf + 1) * 128]),
                              waits=[t_ev])
                tok_pstr[ci % 2] = t_st
                last.append(t_st)
            t_ld_prev = last[-1]
        em.barrier()

        def bcast_row(row_ap, width, ps, waits):
            return em.op("t", lambda e: e.matmul(ps[:, 0:width], s_ones1[:], row_ap, start=True, stop=True),
                         waits=tl(waits, tok_const))

        def softplus_tile(dst, src_ps, width, bias_col, tokens):
            toks = []
            tr = em.op("a", lambda e: e.activation(out=t_s[:, 0:width], in_=src_ps, func=AF.Relu,
                                                   bias=bias_col if bias_col is not None else 0.0),
                       waits=tl(tokens))
            ta = em.op("a", lambda e: e.activation(out=u_s[:, 0:width], in_=src_ps, func=AF.Abs,
                                                   bias=bias_col if bias_col is not None else 0.0),
                       waits=tl(tokens))
            te = em.op("a", lambda e: e.activation(out=u_s[:, 0:width], in_=u_s[:, 0:width], func=AF.Exp, scale=-1.0), waits=[ta])
            tn = em.op("a", lambda e: e.activation(out=u_s[:, 0:width], in_=u_s[:, 0:width], func=AF.Ln, bias=1.0), waits=[te])
            td = em.op("v", lambda e: e.tensor_add(out=dst, in0=t_s[:, 0:width], in1=u_s[:, 0:width]), waits=[tr, tn])
            return td

        def mamba(pre):
            isA = (pre == "a")
            tg = pre
            ms = ExitStack()
            with ms:
                def msb(name, shape, dt=F32):
                    return ms.enter_context(nc.sbuf_tensor(name + "_" + tg, shape, dt))
                m_wx = msb("m_wx", [128, DIC * 128], BF16)

                m_A = msb("m_A", [128, DIC * DS])
                m_convw = msb("m_convw", [128, DIC * 4])
                m_convb = msb("m_convb", [128, DIC])
                m_Dp = msb("m_Dp", [128, DIC])
                m_bdt = msb("m_bdt", [128, DIC])
                m_x3 = msb("m_x3", [128, DC * 3])
                hcar = msb("hcar", [128, DIC * DS])
                xin = [msb(f"xin{i}", [128, TM + 3]) for i in range(2)]
                z_t = msb("z_t", [128, TM])
                xc_t = msb("xc_t", [128, TM])
                xcbf = [msb(f"xcbf{i}", [128, TM], BF16) for i in range(DIC)]
                dtt = [msb(f"dtt{i}", [128, TM]) for i in range(DIC)]
                Bbc = [msb(f"Bbc{i}", [128, TM], BF16) for i in range(DS)]
                Cbc = [msb(f"Cbc{i}", [128, TM], BF16) for i in range(DS)]
                Bsb = msb("Bsb", [DS, TM])
                Csb = msb("Csb", [DS, TM])
                dtx = msb("dtx", [128, TM])
                dA_s = msb("dA_s", [128, TM])
                yac = msb("yac", [128, TM])
                gvec = msb("gvec", [1, 3 * TM])
                rg0 = msb("rg0", [128, DIC * DS])
                rg1 = msb("rg1", [128, DIC * DS])
                ms2 = ExitStack()
                def m2sb(name, shape, dt=F32):
                    return ms2.enter_context(nc.sbuf_tensor(name + "_p1_" + tg, shape, dt))
                m_wdt = m2sb("m_wdt", [64, DI], BF16)
                xnbf = [m2sb(f"xnbf{i}", [128, TM + 3], BF16) for i in range(DC)]
                xi_t = [m2sb(f"xi{i}", [128, TM + 3]) for i in range(2)]
                xi_tail = m2sb("xi_tail", [128, DIC * 3])
                dtrbf = m2sb("dtrbf", [64, TM], BF16)

                t_p0 = em.dma("s", lambda e: e.dma_start(out=wstg[0][:, 0:2048].rearrange("p (kc n) -> p kc n", kc=DIC), in_=W[pre + "Wx"].rearrange("(kc p) n -> p kc n", p=128)))
                t_wx = em.op("p", lambda e: e.tensor_copy(out=m_wx[:], in_=wstg[0][:, 0:DIC * 128]), waits=[t_p0])
                if isA:
                    t_p1 = em.dma("s", lambda e: e.dma_start(out=wstg[1][0:64, 0:DI], in_=W["aWdt"][:]), )
                    t_wdt = em.op("p", lambda e: e.tensor_copy(out=m_wdt[:], in_=wstg[1][0:64, 0:DI]), waits=[t_p1])
                    t_bdt = em.dma("s", lambda e: e.dma_start(out=m_bdt[:], in_=W["abdt"][:]))
                else:
                    t_wdt = t_bdt = None
                t_al = em.dma("s", lambda e: e.dma_start(out=m_A[:], in_=W[pre + "Alog"][:]))
                t_A1 = em.op("a", lambda e: e.activation(out=m_A[:], in_=m_A[:], func=AF.Exp), waits=[t_al])
                t_A = em.op("v", lambda e: e.tensor_scalar(out=m_A[:], in0=m_A[:], scalar1=-1.0, scalar2=None, op0=ALU.mult), waits=[t_A1])
                t_cw = em.dma("s", lambda e: e.dma_start(out=m_convw[:], in_=W[pre + "convw"][:]))
                t_cb = em.dma("s", lambda e: e.dma_start(out=m_convb[:], in_=W[pre + "convb"][:]))
                t_dp = em.dma("s", lambda e: e.dma_start(out=m_Dp[:], in_=W[pre + "Dp"][:]))

                if isA:
                    t_x3 = em.dma("s", lambda e: e.dma_start(out=m_x3[:], in_=xp3a[:]))
                    tok_x3 = [t_x3]
                else:
                    tst = []
                    for ci in range(DC):
                        tst.append(em.dma("p", lambda e, ci=ci: e.dma_start(
                            out=cc_x3_in[ci * 128:(ci + 1) * 128, :], in_=xdr[ci, :, Lc - 3:Lc])))
                    t_cc = em.cc(lambda e: e.collective_compute(
                        "AllGather", ALU.bypass, replica_groups=PAIRS,
                        ins=[cc_x3_in[:, :]], outs=[cc_x3_out[:, :]]), waits=tst)
                    t_l0 = em.dma("s", lambda e: e.dma_start(out=t_s[:, 0:DC * 3].rearrange("p (cc t) -> p cc t", cc=DC), in_=cc_x3_out[0:D, :].rearrange("(cc p) t -> p cc t", p=128)), waits=[t_cc])
                    t_l1 = em.dma("s", lambda e: e.dma_start(out=u_s[:, 0:DC * 3].rearrange("p (cc t) -> p cc t", cc=DC), in_=cc_x3_out[D:2 * D, :].rearrange("(cc p) t -> p cc t", p=128)), waits=[t_cc])
                    t_c0 = em.op("v", lambda e: e.tensor_scalar(out=m_x3[:], in0=t_s[:, 0:DC * 3], scalar1=s_cflag[:, 2:3], scalar2=None, op0=ALU.mult),
                                 waits=tl([t_l0], tok_const))
                    t_c1 = em.op("v", lambda e: e.scalar_tensor_tensor(out=m_x3[:], in0=u_s[:, 0:DC * 3], scalar=s_cflag[:, 3:4], in1=m_x3[:],
                                                                       op0=ALU.mult, op1=ALU.add), waits=[t_l1, t_c0])
                    t_c2 = em.op("v", lambda e: e.tensor_scalar(out=m_x3[:], in0=m_x3[:], scalar1=s_cflag[:, 0:1], scalar2=None, op0=ALU.mult),
                                 waits=[t_c1])
                    tok_x3 = [t_c2]
                t_hz = em.op("v", lambda e: e.memset(hcar[:], 0.0))
                tok_hcar = [t_hz]

                free = {}
                free[("wstg", 0)] = t_wx
                if isA:
                    free[("wstg", 1)] = t_wdt

                def chunk_pass(mc, pass2):
                    nonlocal tok_hcar
                    tc0 = (mc == 0)
                    ext = 3 if (tc0 and not pass2) else 0
                    width = TM + ext
                    col0 = mc * TM
                    if not pass2:
                        t_sq = []
                        for ci in range(DC):
                            xb = xin[ci % 2]
                            t_l = em.dma("s", lambda e, ci=ci, xb=xb: e.dma_start(out=xb[:, 3:3 + TM], in_=xdr[ci, :, col0:col0 + TM]),
                                         waits=tl(free.get(("xin", ci % 2))))
                            tpad = None
                            if ext:
                                tpad = em.op("v", lambda e, ci=ci, xb=xb: e.tensor_copy(out=xb[:, 0:3], in_=m_x3[:, ci * 3:ci * 3 + 3]),
                                             waits=tl(tok_x3, free.get(("xin", ci % 2))))
                            t_q = em.op("a", lambda e, xb=xb: e.activation(out=sq_s[:, 0:width], in_=xb[:, 3 - ext:3 + TM], func=AF.Square),
                                        waits=tl([t_l, tpad], free.get("sq")))
                            t1 = em.op("t", lambda e, ci=ci: e.matmul(psC[0:1, 0:width], s_onesc[:], sq_s[:, 0:width],
                                                                      start=(ci == 0), stop=(ci == DC - 1)),
                                       waits=tl([t_q], tok_const, free.get("psC") if ci == 0 else None))
                            free["sq"] = t1
                            t2 = em.op("t", lambda e, ci=ci, xb=xb: e.matmul(psD[0:1, 0:width], s_onesc[:], xb[:, 3 - ext:3 + TM],
                                                                             start=(ci == 0), stop=(ci == DC - 1)),
                                       waits=tl([t_l, tpad], tok_const, free.get("psD") if ci == 0 else None))
                            free[("xin", ci % 2)] = [t1, t2]
                            t_sq.append([t1, t2])
                        t_mean = em.op("v", lambda e: e.tensor_scalar(out=srow[0:1, 0:width], in0=psD[0:1, 0:width],
                                                                      scalar1=1.0 / D, scalar2=None, op0=ALU.mult),
                                       waits=tl(t_sq[-1]))
                        t_m2 = em.op("v", lambda e: e.tensor_mul(out=srow[0:1, width:2 * width], in0=srow[0:1, 0:width], in1=srow[0:1, 0:width]),
                                     waits=[t_mean])
                        t_var = em.op("v", lambda e: e.scalar_tensor_tensor(out=srow[0:1, width:2 * width], in0=psC[0:1, 0:width],
                                                                            scalar=1.0 / D, in1=srow[0:1, width:2 * width],
                                                                            op0=ALU.mult, op1=ALU.subtract),
                                      waits=tl(t_sq[-1], [t_m2]))
                        t_sd = em.op("a", lambda e: e.activation(out=srow[0:1, width:2 * width], in_=srow[0:1, width:2 * width],
                                                                 func=AF.Sqrt, bias=epsln[0:1, 0:1]), waits=[t_var])
                        t_rs = em.op("v", lambda e: e.reciprocal(out=srow[0:1, 2 * width:3 * width], in_=srow[0:1, width:2 * width]), waits=[t_sd])
                        free["psC"] = t_var
                        free["psD"] = t_mean
                        t_b1 = bcast_row(srow[0:1, 2 * width:3 * width], width, psC, [t_rs, free["psC"]])
                        t_nm = em.op("v", lambda e: e.tensor_mul(out=srow[0:1, 0:width], in0=srow[0:1, 0:width], in1=srow[0:1, 2 * width:3 * width]),
                                     waits=[t_rs, t_mean])
                        t_b2 = bcast_row(srow[0:1, 0:width], width, psD, [t_nm, free["psD"]])
                        t_xn = []
                        for ci in range(DC):
                            xb = xin[ci % 2]
                            t_l = em.dma("s", lambda e, ci=ci, xb=xb: e.dma_start(out=xb[:, 3:3 + TM], in_=xdr[ci, :, col0:col0 + TM]),
                                         waits=tl(free.get(("xin", ci % 2))))
                            tpad = None
                            if ext:
                                tpad = em.op("v", lambda e, ci=ci, xb=xb: e.tensor_copy(out=xb[:, 0:3], in_=m_x3[:, ci * 3:ci * 3 + 3]),
                                             waits=tl(tok_x3, free.get(("xin", ci % 2))))
                            ta = em.op("v", lambda e, xb=xb: e.tensor_mul(out=xb[:, 3 - ext:3 + TM], in0=xb[:, 3 - ext:3 + TM], in1=psC[:, 0:width]),
                                       waits=tl([t_l, tpad, t_b1]))
                            tb = em.op("v", lambda e, ci=ci, xb=xb: e.tensor_sub(out=xnbf[ci][:, 3 - ext:3 + TM], in0=xb[:, 3 - ext:3 + TM], in1=psD[:, 0:width]),
                                       waits=[ta, t_b2])
                            free[("xin", ci % 2)] = [tb]
                            t_xn.append(tb)
                        free["psC"] = t_xn
                        free["psD"] = t_xn
                        t_conv_all = []
                        for j in range(2 * DIC):
                            buf = j % 2
                            t_ldw = em.dma("s", lambda e, j=j, buf=buf: e.dma_start(
                                out=wstg[buf][:, 0:DC * 128].rearrange("p (kc n) -> p kc n", kc=DC),
                                in_=W[pre + "Win"][:, j * 128:(j + 1) * 128].rearrange("(kc p) n -> p kc n", p=128)),
                                waits=tl(free.get(("wstg", buf))))
                            t_cst = em.op("p", lambda e, buf=buf: e.tensor_copy(out=wbfg[buf][:, 0:DC * 128], in_=wstg[buf][:, 0:DC * 128]),
                                          waits=tl([t_ldw], free.get(("wbfg", buf))))
                            free[("wstg", buf)] = t_cst
                            ps = [psA, psB][buf]
                            t_mm = []
                            for kc in range(DC):
                                t_mm.append(em.op("t", lambda e, kc=kc, buf=buf, ps=ps: e.matmul(
                                    ps[:, 0:width], wbfg[buf][:, kc * 128:(kc + 1) * 128], xnbf[kc][:, 3 - ext:3 + TM],
                                    start=(kc == 0), stop=(kc == DC - 1)),
                                    waits=tl([t_cst], t_xn, free.get(("psab", buf)))))
                            free[("wbfg", buf)] = t_mm[-1]
                            if j < DIC:
                                xb = xi_t[buf]
                                if tc0 and ext:
                                    t_ev = em.op("v", lambda e, ps=ps, xb=xb: e.tensor_copy(out=xb[:, 0:3 + TM], in_=ps[:, 0:width]),
                                                 waits=tl(t_mm, free.get(("xi", buf))))
                                else:
                                    t_tl = em.op("v", lambda e, j=j, xb=xb: e.tensor_copy(out=xb[:, 0:3], in_=xi_tail[:, j * 3:(j + 1) * 3]),
                                                 waits=tl(free.get(("xi", buf)), free.get(("tail", j))))
                                    t_ev = em.op("v", lambda e, ps=ps, xb=xb: e.tensor_copy(out=xb[:, 3:3 + TM], in_=ps[:, 0:TM]),
                                                 waits=tl(t_mm, [t_tl]))
                                free[("psab", buf)] = t_ev
                                t_tls = em.op("v", lambda e, j=j, xb=xb: e.tensor_copy(out=xi_tail[:, j * 3:(j + 1) * 3], in_=xb[:, TM:TM + 3]),
                                              waits=[t_ev])
                                free[("tail", j)] = t_tls
                                t1 = em.op("v", lambda e, j=j, xb=xb: e.tensor_scalar(out=xc_t[:], in0=xb[:, 0:TM], scalar1=m_convw[:, j * 4:j * 4 + 1],
                                                                                      scalar2=None, op0=ALU.mult),
                                           waits=tl([t_ev, t_cw], free.get("xc_t")))
                                t2 = em.op("v", lambda e, j=j, xb=xb: e.scalar_tensor_tensor(out=xc_t[:], in0=xb[:, 1:1 + TM], scalar=m_convw[:, j * 4 + 1:j * 4 + 2],
                                                                                             in1=xc_t[:], op0=ALU.mult, op1=ALU.add), waits=[t1])
                                t3 = em.op("v", lambda e, j=j, xb=xb: e.scalar_tensor_tensor(out=xc_t[:], in0=xb[:, 2:2 + TM], scalar=m_convw[:, j * 4 + 2:j * 4 + 3],
                                                                                             in1=xc_t[:], op0=ALU.mult, op1=ALU.add), waits=[t2])
                                t4 = em.op("v", lambda e, j=j, xb=xb: e.scalar_tensor_tensor(out=xc_t[:], in0=xb[:, 3:3 + TM], scalar=m_convw[:, j * 4 + 3:j * 4 + 4],
                                                                                             in1=xc_t[:], op0=ALU.mult, op1=ALU.add), waits=[t3])
                                free[("xi", buf)] = t4
                                t5 = em.op("a", lambda e, j=j: e.activation(out=xc_t[:], in_=xc_t[:], func=AF.Silu, bias=m_convb[:, j:j + 1]),
                                           waits=tl([t4, t_cb]))
                                t_bf = em.op("p", lambda e, j=j: e.tensor_copy(out=xcbf[j][:], in_=xc_t[:]), waits=[t5])
                                t_sp = em.dma("p", lambda e, j=j, mc=mc: e.dma_start(out=sp_xc[mc, j], in_=xc_t[:]), waits=[t5])
                                free["xc_t"] = [t_bf, t_sp]
                                t_conv_all.append(t_bf)
                            else:
                                jz = j - DIC
                                t_ev = em.op("v", lambda e, ps=ps: e.tensor_copy(out=z_t[:], in_=ps[:, ext:ext + TM]),
                                             waits=tl(t_mm, free.get("z_t")))
                                free[("psab", buf)] = t_ev
                                t_spz = em.dma("p", lambda e, jz=jz, mc=mc: e.dma_start(out=sp_z[mc, jz], in_=z_t[:]), waits=[t_ev])
                                free["z_t"] = t_spz
                        t_xp = []
                        for kc in range(DIC):
                            t_xp.append(em.op("t", lambda e, kc=kc: e.matmul(psS[:, 0:TM], m_wx[:, kc * 128:(kc + 1) * 128], xcbf[kc][:],
                                                                             start=(kc == 0), stop=(kc == DIC - 1)),
                                              waits=tl([t_wx], t_conv_all, free.get("psS") if kc == 0 else None)))
                        t_B = em.op("v", lambda e: e.tensor_copy(out=Bsb[:], in_=psS[0:DS, 0:TM]),
                                    waits=tl(t_xp, free.get("Bsb")))
                        t_C = em.op("v", lambda e: e.tensor_copy(out=Csb[:], in_=psS[32:32 + DS, 0:TM]),
                                    waits=tl(t_xp, free.get("Csb")))
                        t_spB = em.dma("p", lambda e, mc=mc: e.dma_start(out=sp_B[mc], in_=Bsb[:]), waits=[t_B])
                        t_spC = em.dma("p", lambda e, mc=mc: e.dma_start(out=sp_C[mc], in_=Csb[:]), waits=[t_C])
                        free["Bsb"] = [t_spB]
                        free["Csb"] = [t_spC]
                        t_dt_all = []
                        if isA:
                            t_dtr = em.op("v", lambda e: e.tensor_copy(out=dtrbf[:], in_=psS[64:128, 0:TM]), waits=tl(t_xp))
                            free["psS"] = [t_B, t_C, t_dtr]
                            for dc2 in range(DIC):
                                tm1 = em.op("t", lambda e, dc2=dc2: e.matmul(psA[:, 0:TM], m_wdt[:, dc2 * 128:(dc2 + 1) * 128], dtrbf[:],
                                                                             start=True, stop=True),
                                            waits=tl([t_wdt, t_dtr], free.get(("psab", 0))))
                                td = softplus_tile(dtt[dc2][:], psA[:, 0:TM], TM, m_bdt[:, dc2:dc2 + 1],
                                                   tl([tm1, t_bdt], free.get("sp_ts"), free.get(("dtt", dc2))))
                                free[("psab", 0)] = td
                                free["sp_ts"] = td
                                t_sp = em.dma("p", lambda e, dc2=dc2, mc=mc: e.dma_start(out=sp_dt[mc, dc2], in_=dtt[dc2][:]), waits=[td])
                                free[("dtt", dc2)] = t_sp
                                t_dt_all.append(td)
                        else:
                            t_d0 = em.op("v", lambda e: e.tensor_copy(out=gvec[0:1, 0:TM], in_=psS[64:65, 0:TM]),
                                         waits=tl(t_xp, free.get("gvec")))
                            free["psS"] = [t_B, t_C, t_d0]
                            tr = em.op("a", lambda e: e.activation(out=gvec[0:1, TM:2 * TM], in_=gvec[0:1, 0:TM], func=AF.Relu), waits=[t_d0])
                            ta = em.op("a", lambda e: e.activation(out=gvec[0:1, 2 * TM:3 * TM], in_=gvec[0:1, 0:TM], func=AF.Abs), waits=[t_d0])
                            te = em.op("a", lambda e: e.activation(out=gvec[0:1, 2 * TM:3 * TM], in_=gvec[0:1, 2 * TM:3 * TM], func=AF.Exp, scale=-1.0), waits=[ta])
                            tn = em.op("a", lambda e: e.activation(out=gvec[0:1, 2 * TM:3 * TM], in_=gvec[0:1, 2 * TM:3 * TM], func=AF.Ln, bias=1.0), waits=[te])
                            td = em.op("v", lambda e: e.tensor_add(out=gvec[0:1, 0:TM], in0=gvec[0:1, TM:2 * TM], in1=gvec[0:1, 2 * TM:3 * TM]), waits=[tr, tn])
                            t_db = bcast_row(gvec[0:1, 0:TM], TM, psC, tl([td], free.get("psC")))
                            t_dbc = em.op("v", lambda e: e.tensor_copy(out=dtt[0][:], in_=psC[:, 0:TM]),
                                          waits=tl([t_db], free.get(("dtt", 0))))
                            free["psC"] = t_dbc
                            t_spd = em.dma("p", lambda e, mc=mc: e.dma_start(out=sp_dr[mc], in_=gvec[0:1, 0:TM]), waits=[td])
                            free["gvec"] = t_spd
                            t_dt_all = [t_dbc]
                        t_bbc = []
                        for s_ in range(DS):
                            ps = [psC, psD][s_ % 2]
                            tmm = em.op("t", lambda e, s_=s_, ps=ps: e.matmul(ps[:, 0:TM], s_csel[:, s_ * 128:(s_ + 1) * 128], Bsb[:],
                                                                             start=True, stop=True),
                                        waits=tl([t_B], tok_const, free.get("psC") if s_ % 2 == 0 else free.get("psD")))
                            tev = em.op("v", lambda e, s_=s_, ps=ps: e.tensor_copy(out=Bbc[s_][:], in_=ps[:, 0:TM]), waits=[tmm])
                            if s_ % 2 == 0:
                                free["psC"] = tev
                            else:
                                free["psD"] = tev
                            free["Bsb"] = tl(free.get("Bsb"), [tmm])
                            t_bbc.append(tev)
                        for cc2 in range(DIC):
                            ci = cc2 if isA else 0
                            t_dx = em.op("v", lambda e, cc2=cc2, ci=ci: e.tensor_mul(out=dtx[:], in0=dtt[ci][:], in1=xcbf[cc2][:]),
                                         waits=tl(t_dt_all, t_conv_all, free.get("dtx")))
                            for s_ in range(DS):
                                t_da = em.op("a", lambda e, cc2=cc2, s_=s_, ci=ci: e.activation(
                                    out=dA_s[:], in_=dtt[ci][:], func=AF.Exp, scale=m_A[:, cc2 * DS + s_:cc2 * DS + s_ + 1]),
                                    waits=tl([t_A], t_dt_all, free.get("dA")))
                                t_u = em.op("v", lambda e, s_=s_: e.tensor_mul(out=u_s[:, 0:TM], in0=dtx[:], in1=Bbc[s_][:]),
                                            waits=tl([t_dx, t_bbc[s_]], free.get("scanu")))
                                t_h = em.op("v", lambda e, cc2=cc2, s_=s_: e.tensor_tensor_scan(
                                    out=h_s[:, 0:TM], data0=dA_s[:], data1=u_s[:, 0:TM],
                                    initial=hcar[:, cc2 * DS + s_:cc2 * DS + s_ + 1],
                                    op0=ALU.mult, op1=ALU.add),
                                    waits=tl([t_da, t_u], tok_hcar, free.get("scanh")))
                                free["dA"] = t_h
                                free["scanu"] = t_h
                                t_cs = em.op("v", lambda e, cc2=cc2, s_=s_: e.tensor_copy(
                                    out=hcar[:, cc2 * DS + s_:cc2 * DS + s_ + 1], in_=h_s[:, TM - 1:TM]), waits=[t_h])
                                free["scanh"] = t_cs
                                tok_hcar = tl(tok_hcar, [t_cs])
                            free["dtx"] = t_h
                            free[("dtt", cc2 if isA else 0)] = tl(free.get(("dtt", cc2 if isA else 0)), [t_h])
                    else:
                        t_dt_all = []
                        if isA:
                            for dc2 in range(DIC):
                                t_dt_all.append(em.dma("s", lambda e, dc2=dc2, mc=mc: e.dma_start(out=dtt[dc2][:], in_=sp_dt[mc, dc2]),
                                                       waits=tl(free.get(("dtt", dc2)))))
                        else:
                            t_ld = em.dma("s", lambda e, mc=mc: e.dma_start(out=gvec[0:1, 0:TM], in_=sp_dr[mc]),
                                          waits=tl(free.get("gvec")))
                            t_db = bcast_row(gvec[0:1, 0:TM], TM, psC, tl([t_ld], free.get("psC")))
                            free["gvec"] = t_db
                            t_dbc = em.op("v", lambda e: e.tensor_copy(out=dtt[0][:], in_=psC[:, 0:TM]), waits=[t_db])
                            free["psC"] = t_dbc
                            t_dt_all = [t_dbc]
                        t_lB = em.dma("s", lambda e, mc=mc: e.dma_start(out=Bsb[:], in_=sp_B[mc]), waits=tl(free.get("Bsb")))
                        t_lC = em.dma("s", lambda e, mc=mc: e.dma_start(out=Csb[:], in_=sp_C[mc]), waits=tl(free.get("Csb")))
                        t_bbc, t_cbc = [], []
                        for s_ in range(DS):
                            ps = [psC, psD][s_ % 2]
                            k1 = "psC" if s_ % 2 == 0 else "psD"
                            tmm = em.op("t", lambda e, s_=s_, ps=ps: e.matmul(ps[:, 0:TM], s_csel[:, s_ * 128:(s_ + 1) * 128], Bsb[:],
                                                                             start=True, stop=True),
                                        waits=tl([t_lB], tok_const, free.get(k1)))
                            tev = em.op("v", lambda e, s_=s_, ps=ps: e.tensor_copy(out=Bbc[s_][:], in_=ps[:, 0:TM]), waits=[tmm])
                            free[k1] = tev
                            free["Bsb"] = tl(free.get("Bsb"), [tmm])
                            t_bbc.append(tev)
                            tmm2 = em.op("t", lambda e, s_=s_, ps=ps: e.matmul(ps[:, 0:TM], s_csel[:, s_ * 128:(s_ + 1) * 128], Csb[:],
                                                                               start=True, stop=True),
                                         waits=tl([t_lC], tok_const, free.get(k1)))
                            tev2 = em.op("v", lambda e, s_=s_, ps=ps: e.tensor_copy(out=Cbc[s_][:], in_=ps[:, 0:TM]), waits=[tmm2])
                            free[k1] = tev2
                            free["Csb"] = tl(free.get("Csb"), [tmm2])
                            t_cbc.append(tev2)
                        t_ybf_all = []
                        for cc2 in range(DIC):
                            ci = cc2 if isA else 0
                            t_lxc = em.dma("s", lambda e, cc2=cc2, mc=mc: e.dma_start(out=xc_t[:], in_=sp_xc[mc, cc2]),
                                           waits=tl(free.get("xc_t")))
                            t_dx = em.op("v", lambda e, cc2=cc2, ci=ci: e.tensor_mul(out=dtx[:], in0=dtt[ci][:], in1=xc_t[:]),
                                         waits=tl(t_dt_all, [t_lxc], free.get("dtx")))
                            t_y = None
                            for s_ in range(DS):
                                t_da = em.op("a", lambda e, cc2=cc2, s_=s_, ci=ci: e.activation(
                                    out=dA_s[:], in_=dtt[ci][:], func=AF.Exp, scale=m_A[:, cc2 * DS + s_:cc2 * DS + s_ + 1]),
                                    waits=tl([t_A], t_dt_all, free.get("dA")))
                                t_u = em.op("v", lambda e, s_=s_: e.tensor_mul(out=u_s[:, 0:TM], in0=dtx[:], in1=Bbc[s_][:]),
                                            waits=tl([t_dx, t_bbc[s_]], free.get("scanu")))
                                t_h = em.op("v", lambda e, cc2=cc2, s_=s_: e.tensor_tensor_scan(
                                    out=h_s[:, 0:TM], data0=dA_s[:], data1=u_s[:, 0:TM],
                                    initial=hcar[:, cc2 * DS + s_:cc2 * DS + s_ + 1],
                                    op0=ALU.mult, op1=ALU.add),
                                    waits=tl([t_da, t_u], tok_hcar, free.get("scanh")))
                                free["dA"] = t_h
                                free["scanu"] = t_h
                                t_cs = em.op("v", lambda e, cc2=cc2, s_=s_: e.tensor_copy(
                                    out=hcar[:, cc2 * DS + s_:cc2 * DS + s_ + 1], in_=h_s[:, TM - 1:TM]), waits=[t_h])
                                tok_hcar = tl(tok_hcar, [t_cs])
                                if s_ == 0:
                                    t_y = em.op("v", lambda e, s_=s_: e.tensor_mul(out=yac[:], in0=h_s[:, 0:TM], in1=Cbc[s_][:]),
                                                waits=tl([t_h, t_cbc[s_]], free.get("yac")))
                                else:
                                    t_t = em.op("v", lambda e, s_=s_: e.tensor_mul(out=t_s[:, 0:TM], in0=h_s[:, 0:TM], in1=Cbc[s_][:]),
                                                waits=tl([t_h, t_cbc[s_]], free.get("sp_ts")))
                                    t_y = em.op("v", lambda e: e.tensor_add(out=yac[:], in0=yac[:], in1=t_s[:, 0:TM]), waits=[t_t, t_y])
                                    free["sp_ts"] = t_y
                                free["scanh"] = t_y if s_ == 0 else t_y
                            t_g1_ = em.op("v", lambda e, cc2=cc2: e.scalar_tensor_tensor(out=yac[:], in0=xc_t[:], scalar=m_Dp[:, cc2:cc2 + 1],
                                                                                         in1=yac[:], op0=ALU.mult, op1=ALU.add),
                                          waits=tl([t_y, t_dp, t_lxc]))
                            free["xc_t"] = t_g1_
                            free["dtx"] = t_g1_
                            t_lz = em.dma("s", lambda e, cc2=cc2, mc=mc: e.dma_start(out=z_t[:], in_=sp_z[mc, cc2]),
                                          waits=tl(free.get("z_t")))
                            t_sz = em.op("a", lambda e: e.activation(out=t_s[:, 0:TM], in_=z_t[:], func=AF.Silu),
                                         waits=tl([t_lz], free.get("sp_ts")))
                            free["z_t"] = t_sz
                            t_yb = em.op("v", lambda e, cc2=cc2: e.tensor_mul(out=xcbf[cc2][:], in0=yac[:], in1=t_s[:, 0:TM]),
                                         waits=[t_g1_, t_sz])
                            free["sp_ts"] = t_yb
                            free["yac"] = t_yb
                            free[("dtt", cc2 if isA else 0)] = t_yb
                            t_ybf_all.append(t_yb)
                        for oc in range(DC):
                            t_mm = []
                            for kc in range(DIC):
                                t_mm.append(em.op("t", lambda e, oc=oc, kc=kc: e.matmul(
                                    psA[:, 0:TM], wout_bf[:, kc * 1024 + oc * 128:kc * 1024 + (oc + 1) * 128], xcbf[kc][:],
                                    start=(kc == 0), stop=(kc == DIC - 1)),
                                    waits=tl(tok_wout, t_ybf_all, free.get(("psab", 0)))))
                            xb = xin[oc % 2]
                            t_lx = em.dma("s", lambda e, oc=oc, mc=mc, xb=xb: e.dma_start(out=xb[:, 0:TM], in_=xdr[oc, :, col0:col0 + TM]),
                                          waits=tl(free.get(("xin", oc % 2))))
                            t_add = em.op("v", lambda e, xb=xb: e.tensor_add(out=xb[:, 0:TM], in0=xb[:, 0:TM], in1=psA[:, 0:TM]),
                                          waits=tl(t_mm, [t_lx]))
                            free[("psab", 0)] = t_add
                            t_st = em.dma("a", lambda e, oc=oc, mc=mc, xb=xb: e.dma_start(out=xdr[oc, :, col0:col0 + TM], in_=xb[:, 0:TM]),
                                          waits=[t_add])
                            free[("xin", oc % 2)] = t_st

                for mc in range(NMC):
                    chunk_pass(mc, False)
                t_hst = em.dma("p", lambda e: e.dma_start(out=cc_h_in[:, :], in_=hcar[:]), waits=tl(tok_hcar))
                t_cc = em.cc(lambda e: e.collective_compute(
                    "AllGather", ALU.bypass, replica_groups=PAIRS,
                    ins=[cc_h_in[:, :]], outs=[cc_h_out[:, :]]), waits=[t_hst])
                t_l0 = em.dma("s", lambda e: e.dma_start(out=rg0[:], in_=cc_h_out[0:128, :]), waits=[t_cc])
                t_l1 = em.dma("s", lambda e: e.dma_start(out=rg1[:], in_=cc_h_out[128:256, :]), waits=[t_cc])
                t_c0 = em.op("v", lambda e: e.tensor_scalar(out=hcar[:], in0=rg0[:], scalar1=s_cflag[:, 2:3], scalar2=None, op0=ALU.mult),
                             waits=tl([t_l0], tok_const, tok_hcar))
                t_c1 = em.op("v", lambda e: e.scalar_tensor_tensor(out=hcar[:], in0=rg1[:], scalar=s_cflag[:, 3:4], in1=hcar[:],
                                                                   op0=ALU.mult, op1=ALU.add), waits=[t_l1, t_c0])
                t_c2 = em.op("v", lambda e: e.tensor_scalar(out=hcar[:], in0=hcar[:], scalar1=s_cflag[:, 0:1], scalar2=None, op0=ALU.mult),
                             waits=[t_c1])
                tok_hcar = [t_c2]
                ms2.close()
                wout_bf = msb("wout_bf", [128, DIC * 1024], BF16)
                tok_wout = []
                for kc in range(DIC):
                    buf = kc % 2
                    t_lw = em.dma("s", lambda e, kc=kc, buf=buf: e.dma_start(out=wstg[buf][:, 0:1024], in_=W[pre + "Wout"][kc * 128:(kc + 1) * 128, :]),
                                  waits=tl(free.get(("wstg", buf))))
                    t_cw2 = em.op("p", lambda e, kc=kc, buf=buf: e.tensor_copy(out=wout_bf[:, kc * 1024:(kc + 1) * 1024], in_=wstg[buf][:, 0:1024]),
                                  waits=[t_lw])
                    free[("wstg", buf)] = t_cw2
                    tok_wout.append(t_cw2)
                em.barrier()
                for mc in range(NMC):
                    chunk_pass(mc, True)
                em.barrier()

        def ffn(idx, parts=("attn", "wo", "moe")):
            tg = f"f{idx}"
            fs = ExitStack()
            with fs:
                def fsb(name, shape, dt=F32):
                    return fs.enter_context(nc.sbuf_tensor(name, shape, dt))
                p1 = ExitStack()
                with p1:
                    xch = [p1.enter_context(nc.sbuf_tensor(f"xch{i}_{tg}", [128, 512], F32)) for i in range(2)]
                    Qbf = [p1.enter_context(nc.sbuf_tensor(f"Qbf{i}_{tg}", [128, Lc], BF16)) for i in range(HQ // 2)]
                    Kctx = [p1.enter_context(nc.sbuf_tensor(f"Kctx{i}_{tg}", [128, 2 * Lc], BF16)) for i in range(HKV * HD // 128)]
                    Vctx = [p1.enter_context(nc.sbuf_tensor(f"Vctx{i}_{tg}", [128, HKV * HD], BF16)) for i in range(2 * NT)]
                    Kown = [Kctx[i][:, 0:Lc] for i in range(HKV * HD // 128)]
                    Vown = [Vctx[i] for i in range(NT)]
                    pA = ExitStack()
                    lnx = [pA.enter_context(nc.sbuf_tensor(f"lnx{i}_{tg}", [128, 512], F32)) for i in range(2)]
                    xn1 = [pA.enter_context(nc.sbuf_tensor(f"xn1_{i}_{tg}", [128, Lc], BF16)) for i in range(DC)]
                    wvbf = pA.enter_context(nc.sbuf_tensor(f"wvbf_{tg}", [128, DC * HKV * HD], BF16))
                    s_cos2 = pA.enter_context(nc.sbuf_tensor(f"cos2_{tg}", [128, Lc], BF16))
                    s_sin2 = pA.enter_context(nc.sbuf_tensor(f"sin2_{tg}", [128, Lc], BF16))
                    free = {}
                    tok_c2 = [em.dma("s", lambda e: e.dma_start(out=s_cos2[:], in_=cos2[:])),
                              em.dma("s", lambda e: e.dma_start(out=s_sin2[:], in_=sin2[:]))]
                    tok_constf = tok_const + tok_c2
                    NCH = Lc // 512
                    for tch in range(NCH):
                        a0 = tch * 512
                        for ci in range(DC):
                            xb = xch[ci % 2]
                            t_l = em.dma("s", lambda e, ci=ci, a0=a0, xb=xb: e.dma_start(out=xb[:], in_=xdr[ci, :, a0:a0 + 512]),
                                         waits=tl(free.get(("xch", ci % 2))))
                            t_q = em.op("a", lambda e, xb=xb: e.activation(out=sq_s[:, 0:512], in_=xb[:], func=AF.Square),
                                        waits=tl([t_l], free.get("sq")))
                            t1 = em.op("t", lambda e, ci=ci: e.matmul(psC[0:1, 0:512], s_onesc[:], sq_s[:, 0:512],
                                                                      start=(ci == 0), stop=(ci == DC - 1)),
                                       waits=tl([t_q], tok_const, free.get("psC") if ci == 0 else None))
                            free["sq"] = t1
                            t2 = em.op("t", lambda e, ci=ci, xb=xb: e.matmul(psD[0:1, 0:512], s_onesc[:], xb[:],
                                                                             start=(ci == 0), stop=(ci == DC - 1)),
                                       waits=tl([t_l], tok_const, free.get("psD") if ci == 0 else None))
                            free[("xch", ci % 2)] = [t1, t2]
                        t_mean = em.op("v", lambda e, a0=a0: e.tensor_scalar(out=stat[0:1, a0:a0 + 512], in0=psD[0:1, 0:512],
                                                                            scalar1=1.0 / D, scalar2=None, op0=ALU.mult),
                                       waits=[free["sq"], free[("xch", (DC - 1) % 2)][1]])
                        t_m2 = em.op("v", lambda e, a0=a0: e.tensor_mul(out=srow[0:1, 0:512], in0=stat[0:1, a0:a0 + 512], in1=stat[0:1, a0:a0 + 512]),
                                     waits=[t_mean])
                        t_var = em.op("v", lambda e: e.scalar_tensor_tensor(out=srow[0:1, 0:512], in0=psC[0:1, 0:512],
                                                                           scalar=1.0 / D, in1=srow[0:1, 0:512],
                                                                           op0=ALU.mult, op1=ALU.subtract),
                                      waits=[free["sq"], t_m2])
                        t_sd = em.op("a", lambda e: e.activation(out=srow[0:1, 0:512], in_=srow[0:1, 0:512], func=AF.Sqrt, bias=epsln[0:1, 0:1]), waits=[t_var])
                        t_rs = em.op("v", lambda e, a0=a0: e.reciprocal(out=stat[0:1, Lc + a0:Lc + a0 + 512], in_=srow[0:1, 0:512]), waits=[t_sd])
                        free["psC"] = t_var
                        free["psD"] = t_mean
                        t_b1 = bcast_row(stat[0:1, Lc + a0:Lc + a0 + 512], 512, psC, tl([t_rs], free.get("psC")))
                        t_nm = em.op("v", lambda e, a0=a0: e.tensor_mul(out=srow[0:1, 0:512], in0=stat[0:1, a0:a0 + 512], in1=stat[0:1, Lc + a0:Lc + a0 + 512]),
                                     waits=[t_rs, t_mean])
                        t_b2 = bcast_row(srow[0:1, 0:512], 512, psD, tl([t_nm], free.get("psD")))
                        t_sq2 = []
                        t_lnl = {}
                        for ci in range(DC):
                            xb = xch[ci % 2]
                            lb = lnx[ci % 2]
                            t_l = em.dma("s", lambda e, ci=ci, a0=a0, xb=xb: e.dma_start(out=xb[:], in_=xdr[ci, :, a0:a0 + 512]),
                                         waits=tl(free.get(("xch", ci % 2))))
                            ta = em.op("v", lambda e, xb=xb, lb=lb: e.tensor_mul(out=lb[:], in0=xb[:], in1=psC[:, 0:512]),
                                       waits=tl([t_l, t_b1], free.get(("lnx", ci % 2))))
                            tb = em.op("v", lambda e, lb=lb: e.tensor_sub(out=lb[:], in0=lb[:], in1=psD[:, 0:512]),
                                       waits=[ta, t_b2])
                            free[("xch", ci % 2)] = tb
                            t_q = em.op("a", lambda e, lb=lb: e.activation(out=sq_s[:, 0:512], in_=lb[:], func=AF.Square),
                                        waits=tl([tb], free.get("sq")))
                            t1 = em.op("t", lambda e, ci=ci: e.matmul(psB[0:1, 0:512], s_onesc[:], sq_s[:, 0:512],
                                                                      start=(ci == 0), stop=(ci == DC - 1)),
                                       waits=tl([t_q], free.get(("psab", 1)) if ci == 0 else None))
                            free["sq"] = t1
                            free[("lnx", ci % 2)] = [t_q]
                            t_sq2.append(t1)
                        t_ms = em.op("v", lambda e: e.tensor_scalar(out=srow[0:1, 0:512], in0=psB[0:1, 0:512], scalar1=1.0 / D,
                                                                    scalar2=None, op0=ALU.mult), waits=tl(t_sq2))
                        t_sd2 = em.op("a", lambda e: e.activation(out=srow[0:1, 0:512], in_=srow[0:1, 0:512], func=AF.Sqrt, bias=epsrm[0:1, 0:1]), waits=[t_ms])
                        t_rr = em.op("v", lambda e: e.reciprocal(out=srow[0:1, 512:1024], in_=srow[0:1, 0:512]), waits=[t_sd2])
                        free[("psab", 1)] = t_ms
                        t_b3 = bcast_row(srow[0:1, 512:1024], 512, psB, tl([t_rr], free.get(("psab", 1))))
                        for ci in range(DC):
                            xb = xch[ci % 2]
                            lb = lnx[ci % 2]
                            t_l = em.dma("s", lambda e, ci=ci, a0=a0, xb=xb: e.dma_start(out=xb[:], in_=xdr[ci, :, a0:a0 + 512]),
                                         waits=tl(free.get(("xch", ci % 2))))
                            ta = em.op("v", lambda e, xb=xb, lb=lb: e.tensor_mul(out=lb[:], in0=xb[:], in1=psC[:, 0:512]),
                                       waits=tl([t_l, t_b1], free.get(("lnx", ci % 2))))
                            tb = em.op("v", lambda e, lb=lb: e.tensor_sub(out=lb[:], in0=lb[:], in1=psD[:, 0:512]),
                                       waits=[ta, t_b2])
                            free[("xch", ci % 2)] = tb
                            tc = em.op("v", lambda e, lb=lb: e.tensor_mul(out=sq_s[:, 0:512], in0=lb[:], in1=psB[:, 0:512]),
                                       waits=tl([tb, t_b3], free.get("sq")))
                            td = em.op("v", lambda e, ci=ci, a0=a0: e.tensor_scalar(out=xn1[ci][:, a0:a0 + 512], in0=sq_s[:, 0:512],
                                                                                    scalar1=s_g1[:, ci:ci + 1], scalar2=None, op0=ALU.mult),
                                       waits=[tc])
                            free["sq"] = td
                            free[("lnx", ci % 2)] = [tc]
                            free[("xn1", ci)] = tl(free.get(("xn1", ci)), [td])
                        free["psC"] = [tb]
                        free["psD"] = [tb]
                        free[("psab", 1)] = [tc]
                    tok_xn1 = tl(*[free[("xn1", ci)] for ci in range(DC)])
                    def proj(wname, ncol, rope, dst_tiles):
                        toks = []
                        for oc in range(ncol // 128):
                            buf = oc % 2
                            t_lw = em.dma("s", lambda e, oc=oc, buf=buf: e.dma_start(
                                out=wstg[buf][:, 0:DC * 128].rearrange("p (kc n) -> p kc n", kc=DC), in_=W[wname][:, oc * 128:(oc + 1) * 128].rearrange("(kc p) n -> p kc n", p=128)),
                                waits=tl(free.get(("wstg", buf))))
                            t_cst = em.op("p", lambda e, buf=buf: e.tensor_copy(out=wbfg[buf][:, 0:DC * 128], in_=wstg[buf][:, 0:DC * 128]),
                                          waits=tl([t_lw], free.get(("wbfg", buf))))
                            free[("wstg", buf)] = t_cst
                            for tch in range(Lc // 512):
                                a0 = tch * 512
                                ps = [psA, psB][tch % 2]
                                kps = ("psab", tch % 2)
                                t_mm = []
                                for kc in range(DC):
                                    t_mm.append(em.op("t", lambda e, oc=oc, kc=kc, a0=a0, buf=buf, ps=ps: e.matmul(
                                        ps[:, 0:512], wbfg[buf][:, kc * 128:(kc + 1) * 128], xn1[kc][:, a0:a0 + 512],
                                        start=(kc == 0), stop=(kc == DC - 1)),
                                        waits=tl([t_cst], tok_xn1, free.get(kps))))
                                if rope:
                                    t_nh = []
                                    prev = free.get("ropenh")
                                    for hb in (0, 64):
                                        t_nh.append(em.op("v", lambda e, hb=hb, ps=ps: e.tensor_scalar(
                                            out=t_s[hb:hb + 32, 0:512], in0=ps[hb + 32:hb + 64, 0:512], scalar1=-1.0, scalar2=None, op0=ALU.mult),
                                            waits=tl(t_mm, [prev])))
                                        t_nh.append(em.op("v", lambda e, hb=hb, ps=ps: e.tensor_copy(
                                            out=t_s[hb + 32:hb + 64, 0:512], in_=ps[hb:hb + 32, 0:512]),
                                            waits=tl(t_mm, [prev])))
                                    t_r1 = em.op("v", lambda e, a0=a0, ps=ps: e.tensor_mul(out=sq_s[:, 0:512], in0=ps[:, 0:512], in1=s_cos2[:, a0:a0 + 512]),
                                                 waits=tl(t_mm, tok_constf, free.get("sq")))
                                    free[kps] = t_r1
                                    t_r2 = em.op("v", lambda e, a0=a0: e.tensor_mul(out=t_s[:, 0:512], in0=t_s[:, 0:512], in1=s_sin2[:, a0:a0 + 512]),
                                                 waits=tl(t_nh, tok_constf))
                                    t_ev = em.op("v", lambda e, oc=oc, a0=a0: e.tensor_add(out=dst_tiles[oc][:, a0:a0 + 512], in0=sq_s[:, 0:512], in1=t_s[:, 0:512]),
                                                 waits=[t_r1, t_r2])
                                    free["ropenh"] = t_ev
                                    free["sq"] = t_ev
                                else:
                                    t_ev = em.op("v", lambda e, oc=oc, a0=a0, ps=ps: e.tensor_copy(out=dst_tiles[oc][:, a0:a0 + 512], in_=ps[:, 0:512]),
                                                 waits=tl(t_mm))
                                    free[kps] = t_ev
                                toks.append(t_ev)
                            free[("wbfg", buf)] = toks[-1]
                        return toks
                    tok_q = proj("fwq", HQ * HD, True, Qbf)
                    tok_k = proj("fwk", HKV * HD, True, Kown)
                    t_lv = em.dma("s", lambda e: e.dma_start(out=wstg[0][:, 0:2048].rearrange("p (kc n) -> p kc n", kc=DC),
                                                             in_=W["fwv"].rearrange("(kc p) n -> p kc n", p=128)),
                                  waits=tl(free.get(("wstg", 0))))
                    t_cv = em.op("p", lambda e: e.tensor_copy(out=wvbf[:], in_=wstg[0][:, 0:DC * HKV * HD]), waits=tl([t_lv], free.get(("wbfg", 0))))
                    free[("wstg", 0)] = t_cv
                    tok_v = []
                    for tt in range(NT):
                        ps = [psA, psB][tt % 2]
                        kps = ("psab", tt % 2)
                        t_mm = []
                        for kc in range(DC):
                            t_mm.append(em.op("t", lambda e, tt=tt, kc=kc, ps=ps: e.matmul(
                                ps[:, 0:HKV * HD], xn1[kc][:, tt * 128:(tt + 1) * 128], wvbf[:, kc * HKV * HD:(kc + 1) * HKV * HD],
                                start=(kc == 0), stop=(kc == DC - 1)),
                                waits=tl([t_cv], tok_xn1, free.get(kps))))
                        t_ev = em.op("v", lambda e, tt=tt, ps=ps: e.tensor_copy(out=Vown[tt][:], in_=ps[:, 0:HKV * HD]), waits=tl(t_mm))
                        free[kps] = t_ev
                        tok_v.append(t_ev)
                    t_kst = [em.dma("p", lambda e, i=i: e.dma_start(out=ccki[idx][i * 128:(i + 1) * 128, :], in_=Kown[i][:]), waits=tok_k)
                             for i in range(HKV * HD // 128)]
                    t_vst = [em.dma("p", lambda e, tt=tt: e.dma_start(out=ccvi[idx][tt * 128:(tt + 1) * 128, :], in_=Vown[tt][:]), waits=tl(tok_v))
                             for tt in range(NT)]
                    t_cck = em.cc(lambda e: e.collective_compute("AllGather", ALU.bypass, replica_groups=PAIRS,
                                                                 ins=[ccki[idx][:, :]], outs=[ccko[idx][:, :]]), waits=tl(t_kst))
                    t_ccv = em.cc(lambda e: e.collective_compute("AllGather", ALU.bypass, replica_groups=PAIRS,
                                                                 ins=[ccvi[idx][:, :]], outs=[ccvo[idx][:, :]]), waits=tl(t_vst))
                    state_attn = {"tok_q": tok_q, "tok_k": tok_k, "tok_v": tok_v, "t_cck": t_cck, "t_ccv": t_ccv}
                    em.barrier()
                    pA.close()

                    P1 = p1.enter_context(nc.sbuf_tensor(f"P1_{tg}", [128, Lc], BF16))
                    P2 = p1.enter_context(nc.sbuf_tensor(f"P2_{tg}", [128, Lc], BF16))
                    PT = [p1.enter_context(nc.sbuf_tensor(f"PT{i}_{tg}", [128, 128], BF16)) for i in range(4)]
                    attnHV = [p1.enter_context(nc.sbuf_tensor(f"attnHV{i}_{tg}", [128, Lc], BF16)) for i in range(HQ * HD // 128)]
                    kvg = [p1.enter_context(nc.sbuf_tensor(f"kvg{i}_{tg}", [128, Lc], BF16)) for i in range(1)]

                    tok_kctx, tok_vctx = [], []
                    for i in range(HKV * HD // 128):
                        t_own = None
                        t_g0 = em.dma("s", lambda e, i=i: e.dma_start(out=kvg[0][:], in_=ccko[idx][i * 128:(i + 1) * 128, :]),
                                      waits=tl([t_cck], free.get("kvg0")))
                        tc0 = em.op("v", lambda e, i=i: e.tensor_scalar(out=Kctx[i][:, Lc:2 * Lc], in0=kvg[0][:], scalar1=s_cflag[:, 2:3],
                                                                        scalar2=None, op0=ALU.mult), waits=tl([t_g0], tok_const))
                        t_g1_ = em.dma("s", lambda e, i=i: e.dma_start(out=kvg[0][:], in_=ccko[idx][HKV * HD + i * 128:HKV * HD + (i + 1) * 128, :]),
                                       waits=tl([t_cck], [tc0]))
                        tc1 = em.op("v", lambda e, i=i: e.scalar_tensor_tensor(out=Kctx[i][:, Lc:2 * Lc], in0=kvg[0][:], scalar=s_cflag[:, 3:4],
                                                                               in1=Kctx[i][:, Lc:2 * Lc], op0=ALU.mult, op1=ALU.add),
                                    waits=[t_g1_, tc0])
                        free["kvg0"] = tc1
                        tok_kctx += tl(tok_k, [tc1])
                    for tt in range(NT):
                        t_own = None
                        t_g0 = em.dma("s", lambda e, tt=tt: e.dma_start(out=kvg[0][:, 0:HKV * HD], in_=ccvo[idx][tt * 128:(tt + 1) * 128, :]),
                                      waits=tl([t_ccv], free.get("kvg0")))
                        tc0 = em.op("v", lambda e, tt=tt: e.tensor_scalar(out=Vctx[NT + tt][:], in0=kvg[0][:, 0:HKV * HD], scalar1=s_cflag[:, 2:3],
                                                                          scalar2=None, op0=ALU.mult), waits=tl([t_g0], tok_const))
                        t_g1_ = em.dma("s", lambda e, tt=tt: e.dma_start(out=kvg[0][:, 0:HKV * HD], in_=ccvo[idx][Lc + tt * 128:Lc + (tt + 1) * 128, :]),
                                       waits=tl([t_ccv], [tc0]))
                        tc1 = em.op("v", lambda e, tt=tt: e.scalar_tensor_tensor(out=Vctx[NT + tt][:], in0=kvg[0][:, 0:HKV * HD], scalar=s_cflag[:, 3:4],
                                                                                 in1=Vctx[NT + tt][:], op0=ALU.mult, op1=ALU.add),
                                    waits=[t_g1_, tc0])
                        free["kvg0"] = tc1
                        tok_vctx += tl(tok_v, [tc1])
                    scale = 1.0 / 8.0
                    for head in (range(HQ) if "attn" in parts else []):
                        kv = head % HKV
                        kvc, kvb = kv // 2, (kv % 2) * 64
                        qc, qb = head // 2, (head % 2) * 64
                        for qt in range(NT):
                            nob = qt // c.KBQ + 1
                            t_smm = []
                            for kb in range(nob):
                                t_smm.append(em.op("t", lambda e, kb=kb, qt=qt, qc=qc, qb=qb, kvc=kvc, kvb=kvb: e.matmul(
                                    psS[:, kb * c.KB:(kb + 1) * c.KB],
                                    Qbf[qc][qb:qb + 64, qt * 128:(qt + 1) * 128],
                                    Kctx[kvc][kvb:kvb + 64, kb * c.KB:(kb + 1) * c.KB],
                                    start=True, stop=True),
                                    waits=tl(tok_q, tok_kctx, free.get(("psSblk", kb)))))
                            wA = nob * c.KB
                            t_cpA = em.op("a", lambda e, wA=wA: e.activation(out=P1[:, 0:wA], in_=psS[:, 0:wA], func=AF.Identity),
                                          waits=tl(t_smm, free.get("P1")))
                            t_m1 = em.op("v", lambda e, wA=wA: e.reduce_max(out=small[:, 0:1], in_=P1[:, 0:wA], axis=mybir.AxisListType.X),
                                         waits=tl([t_cpA], free.get("sm_m1")))
                            t_mb = em.op("v", lambda e: e.tensor_scalar(out=small[:, 2:3], in0=small[:, 0:1], scalar1=-scale, scalar2=None, op0=ALU.mult),
                                         waits=[t_m1])
                            t_e1 = em.op("a", lambda e, wA=wA: e.activation(out=P1[:, 0:wA], in_=psS[:, 0:wA], func=AF.Exp,
                                                                            bias=small[:, 2:3], scale=scale),
                                         waits=tl(t_smm, [t_mb, t_m1]))
                            for kb in range(nob):
                                free[("psSblk", kb)] = t_e1
                            mi = qt % c.KBQ
                            t_msk = em.op("v", lambda e, qt=qt, mi=mi, nob=nob: e.tensor_mul(
                                out=P1[:, (nob - 1) * c.KB:nob * c.KB], in0=P1[:, (nob - 1) * c.KB:nob * c.KB],
                                in1=s_cmaskb[:, mi * c.KB:(mi + 1) * c.KB]), waits=tl([t_e1], tok_const))
                            t_l1 = em.op("v", lambda e, wA=wA: e.reduce_sum(out=small[:, 4:5], in_=P1[:, 0:wA], axis=mybir.AxisListType.X),
                                         waits=tl([t_msk], free.get("sm_m1")))
                            t_pvA = []
                            for i in range(nob * c.KBQ):
                                t_tr = em.dma("a", lambda e, i=i: e.dma_start(out=PT[i % 4][:], in_=P1[:, i * 128:(i + 1) * 128], transpose=True),
                                              waits=tl([t_msk], free.get(("PT", i % 4))))
                                t_pv = em.op("t", lambda e, i=i, kv=kv: e.matmul(psA[:, 0:64], PT[i % 4][:], Vctx[i][:, kv * 64:(kv + 1) * 64],
                                                                                 start=(i == 0), stop=(i == nob * c.KBQ - 1)),
                                             waits=tl([t_tr], tok_vctx, free.get("psA_O") if i == 0 else None))
                                free[("PT", i % 4)] = t_pv
                                t_pvA.append(t_pv)
                            free["P1"] = tl([t_pvA[-1], t_l1])
                            t_smm2 = []
                            for kb in range(c.NKB):
                                t_smm2.append(em.op("t", lambda e, kb=kb, qt=qt, qc=qc, qb=qb, kvc=kvc, kvb=kvb: e.matmul(
                                    psS[:, kb * c.KB:(kb + 1) * c.KB],
                                    Qbf[qc][qb:qb + 64, qt * 128:(qt + 1) * 128],
                                    Kctx[kvc][kvb:kvb + 64, Lc + kb * c.KB:Lc + (kb + 1) * c.KB],
                                    start=True, stop=True),
                                    waits=tl(tok_q, tok_kctx, [t_e1], free.get(("psSblk", kb)))))
                            t_cpB = em.op("a", lambda e: e.activation(out=P2[:, 0:Lc], in_=psS[:, 0:Lc], func=AF.Identity),
                                          waits=tl(t_smm2, free.get("P2")))
                            t_m2 = em.op("v", lambda e: e.reduce_max(out=small[:, 1:2], in_=P2[:, 0:Lc], axis=mybir.AxisListType.X),
                                         waits=tl([t_cpB, t_m1]))
                            t_mb2a = em.op("v", lambda e: e.tensor_scalar(out=small[:, 3:4], in0=small[:, 1:2], scalar1=-scale, scalar2=None, op0=ALU.mult),
                                           waits=[t_m2])
                            t_mb2 = em.op("v", lambda e: e.tensor_add(out=small[:, 3:4], in0=small[:, 3:4], in1=s_cflag[:, 1:2]),
                                          waits=tl([t_mb2a], tok_const))
                            t_e2 = em.op("a", lambda e: e.activation(out=P2[:, 0:Lc], in_=psS[:, 0:Lc], func=AF.Exp,
                                                                     bias=small[:, 3:4], scale=scale),
                                         waits=tl(t_smm2, [t_mb2, t_m2]))
                            for kb in range(c.NKB):
                                free[("psSblk", kb)] = t_e2
                            t_l2 = em.op("v", lambda e: e.reduce_sum(out=small[:, 5:6], in_=P2[:, 0:Lc], axis=mybir.AxisListType.X),
                                         waits=tl([t_e2], free.get("sm_m1")))
                            t_pvB = []
                            for i in range(NT):
                                t_tr = em.dma("a", lambda e, i=i: e.dma_start(out=PT[i % 4][:], in_=P2[:, i * 128:(i + 1) * 128], transpose=True),
                                              waits=tl([t_e2], free.get(("PT", i % 4))))
                                t_pv = em.op("t", lambda e, i=i, kv=kv: e.matmul(psC[:, 0:64], PT[i % 4][:], Vctx[NT + i][:, kv * 64:(kv + 1) * 64],
                                                                                 start=(i == 0), stop=(i == NT - 1)),
                                             waits=tl([t_tr], tok_vctx, free.get("psA_O2") if i == 0 else None))
                                free[("PT", i % 4)] = t_pv
                                t_pvB.append(t_pv)
                            free["P2"] = tl([t_pvB[-1], t_l2])
                            t_m = em.op("v", lambda e: e.tensor_max(out=small[:, 6:7], in0=small[:, 0:1], in1=small[:, 1:2]),
                                        waits=[t_m1, t_m2, free.get("sm_fin")] if free.get("sm_fin") else [t_m1, t_m2])
                            t_d1 = em.op("v", lambda e: e.tensor_sub(out=small[:, 8:9], in0=small[:, 0:1], in1=small[:, 6:7]), waits=[t_m])
                            t_d2 = em.op("v", lambda e: e.tensor_sub(out=small[:, 9:10], in0=small[:, 1:2], in1=small[:, 6:7]), waits=[t_m])
                            t_a1 = em.op("a", lambda e: e.activation(out=small[:, 10:11], in_=small[:, 8:9], func=AF.Exp, scale=scale), waits=[t_d1])
                            t_a2 = em.op("a", lambda e: e.activation(out=small[:, 11:12], in_=small[:, 9:10], func=AF.Exp, scale=scale), waits=[t_d2])
                            t_ll = em.op("v", lambda e: e.tensor_mul(out=small[:, 12:13], in0=small[:, 4:5], in1=small[:, 10:11]), waits=[t_l1, t_a1])
                            t_ll2 = em.op("v", lambda e: e.scalar_tensor_tensor(out=small[:, 12:13], in0=small[:, 5:6], scalar=small[:, 11:12],
                                                                                in1=small[:, 12:13], op0=ALU.mult, op1=ALU.add),
                                          waits=[t_l2, t_a2, t_ll])
                            t_ri = em.op("v", lambda e: e.reciprocal(out=small[:, 13:14], in_=small[:, 12:13]), waits=[t_ll2])
                            t_o1 = em.op("v", lambda e: e.tensor_scalar(out=sq_s[:, 0:64], in0=psA[:, 0:64], scalar1=small[:, 10:11],
                                                                        scalar2=None, op0=ALU.mult),
                                         waits=tl(t_pvA, [t_a1], free.get("sq")))
                            free["psA_O"] = t_o1
                            t_o2 = em.op("v", lambda e: e.scalar_tensor_tensor(out=sq_s[:, 0:64], in0=psC[:, 0:64], scalar=small[:, 11:12],
                                                                               in1=sq_s[:, 0:64], op0=ALU.mult, op1=ALU.add),
                                         waits=tl(t_pvB, [t_a2, t_o1]))
                            free["psA_O2"] = t_o2
                            t_of = em.op("v", lambda e: e.tensor_scalar(out=sq_s[:, 0:64], in0=sq_s[:, 0:64], scalar1=small[:, 13:14],
                                                                        scalar2=None, op0=ALU.mult), waits=[t_ri, t_o2])
                            free["sm_fin"] = t_of
                            free["sm_m1"] = t_of
                            t_tro = em.op("t", lambda e: e.transpose(psB[0:64, 0:128], sq_s[:, 0:64], s_ident[:]),
                                          waits=tl([t_of], free.get("psB_O")))
                            t_evo = em.op("v", lambda e, head=head, qt=qt, qc=qc, qb=qb: e.tensor_copy(
                                out=attnHV[qc][qb:qb + 64, qt * 128:(qt + 1) * 128], in_=psB[0:64, 0:128]), waits=[t_tro])
                            free["psB_O"] = t_evo
                            free["sq"] = t_of
                            free[("attnHV", qc)] = tl(free.get(("attnHV", qc)), [t_evo])
                    tok_attn = tl(*[free.get(("attnHV", i)) for i in range(HQ // 2)])
                    em.barrier()
                    for oc in (range(DC) if "wo" in parts else []):
                        buf = oc % 2
                        t_lw = em.dma("s", lambda e, oc=oc, buf=buf: e.dma_start(
                            out=wstg[buf][:, 0:HQ * HD].rearrange("p (kc n) -> p kc n", kc=HQ * HD // 128), in_=W["fwo"][:, oc * 128:(oc + 1) * 128].rearrange("(kc p) n -> p kc n", p=128)),
                            waits=tl(free.get(("wstg", buf))))
                        t_cst = em.op("p", lambda e, buf=buf: e.tensor_copy(out=wbfg[buf][:, 0:HQ * HD], in_=wstg[buf][:, 0:HQ * HD]),
                                      waits=tl([t_lw], free.get(("wbfg", buf))))
                        free[("wstg", buf)] = t_cst
                        for tch in range(Lc // 512):
                            a0 = tch * 512
                            ps = [psA, psB][tch % 2]
                            kps = ("psab", tch % 2)
                            t_mm = []
                            for kc in range(HQ * HD // 128):
                                t_mm.append(em.op("t", lambda e, kc=kc, a0=a0, buf=buf, ps=ps: e.matmul(
                                    ps[:, 0:512], wbfg[buf][:, kc * 128:(kc + 1) * 128], attnHV[kc][:, a0:a0 + 512],
                                    start=(kc == 0), stop=(kc == HQ * HD // 128 - 1)),
                                    waits=tl([t_cst], tok_attn, free.get(kps))))
                            xb = xch[tch % 2]
                            t_l = em.dma("s", lambda e, oc=oc, a0=a0, xb=xb: e.dma_start(out=xb[:], in_=xdr[oc, :, a0:a0 + 512]),
                                         waits=tl(free.get(("xch", tch % 2))))
                            t_br = bcast_row(stat[0:1, Lc + a0:Lc + a0 + 512], 512, psC, tl(free.get("psC")))
                            t_bm = bcast_row(stat[0:1, a0:a0 + 512], 512, psD, tl(free.get("psD")))
                            ta = em.op("v", lambda e, xb=xb: e.tensor_sub(out=xb[:], in0=xb[:], in1=psD[:, 0:512]), waits=[t_l, t_bm])
                            tb = em.op("v", lambda e, xb=xb: e.tensor_mul(out=xb[:], in0=xb[:], in1=psC[:, 0:512]), waits=[ta, t_br])
                            free["psC"] = tb
                            free["psD"] = ta
                            th = em.op("v", lambda e, xb=xb, ps=ps: e.tensor_add(out=xb[:], in0=xb[:], in1=ps[:, 0:512]),
                                       waits=tl(t_mm, [tb]))
                            free[kps] = th
                            t_st = em.dma("a", lambda e, oc=oc, a0=a0, xb=xb: e.dma_start(out=hdr[oc, :, a0:a0 + 512], in_=xb[:]), waits=[th])
                            free[("xch", tch % 2)] = t_st
                    em.barrier()
                if "moe" not in parts:
                    return
                p3 = ExitStack()
                with p3:
                    xfbf = [p3.enter_context(nc.sbuf_tensor(f"xfbf{i}_{tg}", [128, Lc], BF16)) for i in range(DC)]
                    ysel = [p3.enter_context(nc.sbuf_tensor(f"ysel{i}_{tg}", [128, Lc], BF16)) for i in range(DC)]
                    hgbf = [p3.enter_context(nc.sbuf_tensor(f"hgbf{i}_{tg}", [128, c.TH], BF16)) for i in range(c.HC)]
                    xch3 = [p3.enter_context(nc.sbuf_tensor(f"xch3_{i}_{tg}", [128, 512], F32) ) for i in range(2)]
                    ohst = p3.enter_context(nc.sbuf_tensor(f"ohst_{tg}", [128, NT * 2 * E], F32))
                    mcols = p3.enter_context(nc.sbuf_tensor(f"mcols_{tg}", [128, NT], F32))
                    acols = p3.enter_context(nc.sbuf_tensor(f"acols_{tg}", [128, NT], F32))
                    bcols = p3.enter_context(nc.sbuf_tensor(f"bcols_{tg}", [128, NT], F32))
                    mrows = p3.enter_context(nc.sbuf_tensor(f"mrows_{tg}", [NT, 128], F32))
                    arows = p3.enter_context(nc.sbuf_tensor(f"arows_{tg}", [NT, 128], F32))
                    brows = p3.enter_context(nc.sbuf_tensor(f"brows_{tg}", [NT, 128], F32))
                    merows = p3.enter_context(nc.sbuf_tensor(f"merows_{tg}", [NT, 128], F32))
                    runsb = p3.enter_context(nc.sbuf_tensor(f"runsb_{tg}", [1, 2 * E], F32))
                    rt = p3.enter_context(nc.sbuf_tensor(f"rt_{tg}", [128, 8 * E], F32))
                    frwbf = p3.enter_context(nc.sbuf_tensor(f"frwbf_{tg}", [128, DC * E], BF16))
                    msk = p3.enter_context(nc.sbuf_tensor(f"msk_{tg}", [128, 512], F32))
                    gsb = p3.enter_context(nc.sbuf_tensor(f"gsb_{tg}", [128, 512], F32))
                    free = {}
                    NCH = Lc // 512
                    for tch in range(NCH):
                        a0 = tch * 512
                        t_sq2 = []
                        for ci in range(DC):
                            xb = xch3[ci % 2]
                            t_l = em.dma("s", lambda e, ci=ci, a0=a0, xb=xb: e.dma_start(out=xb[:], in_=hdr[ci, :, a0:a0 + 512]),
                                         waits=tl(free.get(("xch", ci % 2))))
                            t_q = em.op("a", lambda e, xb=xb: e.activation(out=sq_s[:, 0:512], in_=xb[:], func=AF.Square),
                                        waits=tl([t_l], free.get("sq")))
                            t1 = em.op("t", lambda e, ci=ci: e.matmul(psC[0:1, 0:512], s_onesc[:], sq_s[:, 0:512],
                                                                      start=(ci == 0), stop=(ci == DC - 1)),
                                       waits=tl([t_q], tok_const, free.get("psC") if ci == 0 else None))
                            free["sq"] = t1
                            free[("xch", ci % 2)] = t1
                            t_sq2.append(t1)
                        t_ms = em.op("v", lambda e: e.tensor_scalar(out=srow[0:1, 0:512], in0=psC[0:1, 0:512], scalar1=1.0 / D,
                                                                    scalar2=None, op0=ALU.mult), waits=tl(t_sq2))
                        t_sd2 = em.op("a", lambda e: e.activation(out=srow[0:1, 0:512], in_=srow[0:1, 0:512], func=AF.Sqrt, bias=epsrm[0:1, 0:1]), waits=[t_ms])
                        t_rr = em.op("v", lambda e, a0=a0: e.reciprocal(out=stat[0:1, 2 * Lc + a0:2 * Lc + a0 + 512], in_=srow[0:1, 0:512]), waits=[t_sd2])
                        free["psC"] = t_ms
                        t_b3 = bcast_row(stat[0:1, 2 * Lc + a0:2 * Lc + a0 + 512], 512, psC, tl([t_rr], free.get("psC")))
                        for ci in range(DC):
                            xb = xch3[ci % 2]
                            t_l = em.dma("s", lambda e, ci=ci, a0=a0, xb=xb: e.dma_start(out=xb[:], in_=hdr[ci, :, a0:a0 + 512]),
                                         waits=tl(free.get(("xch", ci % 2))))
                            ta = em.op("v", lambda e, xb=xb: e.tensor_mul(out=sq_s[:, 0:512], in0=xb[:], in1=psC[:, 0:512]),
                                       waits=tl([t_l, t_b3], free.get("sq")))
                            tb = em.op("v", lambda e, ci=ci, a0=a0: e.tensor_scalar(out=xfbf[ci][:, a0:a0 + 512], in0=sq_s[:, 0:512],
                                                                                    scalar1=s_g2[:, ci:ci + 1], scalar2=None, op0=ALU.mult),
                                       waits=[ta])
                            free["sq"] = tb
                            free[("xch", ci % 2)] = tb
                            free[("xf", ci)] = tl(free.get(("xf", ci)), [tb])
                        free["psC"] = tb
                    tok_xf = tl(*[free[("xf", ci)] for ci in range(DC)])
                    t_lr = em.dma("s", lambda e: e.dma_start(out=wstg[0][:, 0:DC * E].rearrange("p (kc n) -> p kc n", kc=DC), in_=W["frw"].rearrange("(kc p) n -> p kc n", p=128)),
                                  waits=tl(free.get(("wstg", 0))))
                    t_cr = em.op("p", lambda e: e.tensor_copy(out=frwbf[:], in_=wstg[0][:, 0:DC * E]), waits=[t_lr])
                    free[("wstg", 0)] = t_cr
                    t_tot_mm = []
                    for tt in range(NT):
                        t_mm = []
                        for kc in range(DC):
                            t_mm.append(em.op("t", lambda e, tt=tt, kc=kc: e.matmul(
                                psS[:, 0:E], xfbf[kc][:, tt * 128:(tt + 1) * 128], frwbf[:, kc * E:(kc + 1) * E],
                                start=(kc == 0), stop=(kc == DC - 1)),
                                waits=tl([t_cr], tok_xf, free.get("psSr")))),
                        t_lg = em.op("v", lambda e: e.tensor_add(out=rt[:, 0:E], in0=psS[:, 0:E], in1=s_frb[:]),
                                     waits=tl(t_mm, tok_const, free.get("rt")))
                        free["psSr"] = t_lg
                        t_v1 = em.op("v", lambda e: e.reduce_max(out=small[:, 0:1], in_=rt[:, 0:E], axis=mybir.AxisListType.X), waits=[t_lg])
                        t_o1 = em.op("v", lambda e, tt=tt: e.tensor_scalar(out=ohst[:, tt * 2 * E:tt * 2 * E + E], in0=rt[:, 0:E],
                                                                           scalar1=small[:, 0:1], scalar2=None, op0=ALU.is_equal),
                                     waits=[t_v1])
                        t_l2 = em.op("v", lambda e, tt=tt: e.scalar_tensor_tensor(out=rt[:, E:2 * E], in0=ohst[:, tt * 2 * E:tt * 2 * E + E],
                                                                                  scalar=NEG, in1=rt[:, 0:E], op0=ALU.mult, op1=ALU.add),
                                     waits=[t_o1])
                        t_v2 = em.op("v", lambda e: e.reduce_max(out=small[:, 1:2], in_=rt[:, E:2 * E], axis=mybir.AxisListType.X), waits=[t_l2])
                        t_o2 = em.op("v", lambda e, tt=tt: e.tensor_scalar(out=ohst[:, tt * 2 * E + E:(tt + 1) * 2 * E], in0=rt[:, E:2 * E],
                                                                           scalar1=small[:, 1:2], scalar2=None, op0=ALU.is_equal),
                                     waits=[t_v2])
                        t_dd = em.op("v", lambda e: e.tensor_sub(out=small[:, 2:3], in0=small[:, 0:1], in1=small[:, 1:2]), waits=[t_v1, t_v2])
                        t_s1 = em.op("a", lambda e: e.activation(out=small[:, 3:4], in_=small[:, 2:3], func=AF.Sigmoid), waits=[t_dd])
                        t_s2 = em.op("v", lambda e: e.tensor_scalar(out=small[:, 4:5], in0=small[:, 3:4], scalar1=-1.0, scalar2=1.0,
                                                                    op0=ALU.mult, op1=ALU.add), waits=[t_s1])
                        free["rt"] = [t_v2, t_o2]
                        t_tm = em.op("t", lambda e, tt=tt: e.matmul(psD[0:1, 0:2 * E], s_onesc[:], ohst[:, tt * 2 * E:(tt + 1) * 2 * E],
                                                                    start=(tt == 0), stop=(tt == NT - 1)),
                                     waits=tl([t_o1, t_o2], free.get("psDr") if tt == 0 else None))
                        t_tot_mm.append(t_tm)
                        t_sa = em.op("v", lambda e, tt=tt: e.tensor_copy(out=acols[:, tt:tt + 1], in_=small[:, 3:4]), waits=[t_s1])
                        t_sb = em.op("v", lambda e, tt=tt: e.tensor_copy(out=bcols[:, tt:tt + 1], in_=small[:, 4:5]), waits=[t_s2])
                        free[("sc", tt)] = [t_sa, t_sb]
                    t_tots = em.op("v", lambda e: e.tensor_copy(out=runsb[0:1, 0:2 * E], in_=psD[0:1, 0:2 * E]), waits=tl(t_tot_mm))
                    free["psDr"] = t_tots
                    t_std = em.dma("p", lambda e: e.dma_start(out=cc_r_in[idx][:, :], in_=runsb[0:1, 0:2 * E]),
                                   waits=[t_tots])
                    t_ccr = em.cc(lambda e: e.collective_compute("AllGather", ALU.bypass, replica_groups=ALL8,
                                                                 ins=[cc_r_in[idx][:, :]], outs=[cc_r_out[idx][:, :]]), waits=[t_std])
                    t_lg8 = em.dma("s", lambda e: e.dma_start(out=rt[0:8, 0:2 * E], in_=cc_r_out[idx][:, :]), waits=[t_ccr])
                    t_pr0 = em.op("t", lambda e: e.matmul(psD[0:1, 0:2 * E], s_cpred[:, 0:1], rt[0:8, 0:2 * E], start=True, stop=True),
                                  waits=tl([t_lg8], tok_const, [t_tots]))
                    t_run = em.op("v", lambda e: e.tensor_copy(out=runsb[0:1, 0:2 * E], in_=psD[0:1, 0:2 * E]), waits=[t_pr0, t_tots])
                    free["psDr"] = t_run
                    tok_run = [t_run]
                    for tt in range(NT):
                        t_c1m = em.op("t", lambda e, tt=tt: e.matmul(psS[:, 0:2 * E], s_ctri[:], ohst[:, tt * 2 * E:(tt + 1) * 2 * E],
                                                                     start=True, stop=False),
                                      waits=tl(tok_const, free.get("psSr"))),
                        t_c1b = em.op("t", lambda e: e.matmul(psS[:, 0:2 * E], s_ones1[:], runsb[0:1, 0:2 * E], start=False, stop=True),
                                      waits=tl(tok_run))
                        t_cp0 = em.op("v", lambda e: e.tensor_copy(out=rt[:, 7 * E:8 * E], in_=psS[:, 0:E]),
                                      waits=tl([t_c1b], free.get("rt")))
                        t_p1_ = em.op("v", lambda e: e.tensor_add(out=rt[:, 0:E], in0=rt[:, 7 * E:8 * E], in1=psS[:, E:2 * E]),
                                      waits=tl([t_c1b, t_cp0]))
                        t_lt0 = em.op("v", lambda e: e.tensor_scalar(out=rt[:, E:2 * E], in0=psS[:, 0:E], scalar1=float(c.cap),
                                                                     scalar2=None, op0=ALU.is_lt), waits=[t_c1b])
                        t_lt1 = em.op("v", lambda e: e.tensor_scalar(out=rt[:, 2 * E:3 * E], in0=rt[:, 0:E], scalar1=float(c.cap),
                                                                     scalar2=None, op0=ALU.is_lt), waits=[t_p1_])
                        free["psSr"] = [t_lt0, t_lt1]
                        t_em0 = em.op("v", lambda e, tt=tt: e.tensor_mul(out=rt[:, 3 * E:4 * E], in0=ohst[:, tt * 2 * E:tt * 2 * E + E],
                                                                         in1=rt[:, E:2 * E]), waits=[t_lt0])
                        t_em1 = em.op("v", lambda e, tt=tt: e.tensor_mul(out=rt[:, 4 * E:5 * E], in0=ohst[:, tt * 2 * E + E:(tt + 1) * 2 * E],
                                                                         in1=rt[:, 2 * E:3 * E]), waits=[t_lt1])
                        t_tm0 = em.op("v", lambda e: e.reduce_sum(out=small[:, 0:1], in_=rt[:, 3 * E:4 * E], axis=mybir.AxisListType.X), waits=[t_em0])
                        t_tm1 = em.op("v", lambda e: e.reduce_sum(out=small[:, 1:2], in_=rt[:, 4 * E:5 * E], axis=mybir.AxisListType.X), waits=[t_em1])
                        t_as0 = em.op("v", lambda e: e.tensor_mul(out=rt[:, 5 * E:6 * E], in0=rt[:, 3 * E:4 * E], in1=s_cioE[:]),
                                      waits=tl([t_em0], tok_const))
                        t_as1 = em.op("v", lambda e: e.tensor_mul(out=rt[:, 6 * E:7 * E], in0=rt[:, 4 * E:5 * E], in1=s_cioE[:]),
                                      waits=tl([t_em1], tok_const))
                        t_a0 = em.op("v", lambda e: e.reduce_sum(out=small[:, 2:3], in_=rt[:, 5 * E:6 * E], axis=mybir.AxisListType.X), waits=[t_as0])
                        t_a1 = em.op("v", lambda e: e.reduce_sum(out=small[:, 3:4], in_=rt[:, 6 * E:7 * E], axis=mybir.AxisListType.X), waits=[t_as1])
                        t_mv = em.op("v", lambda e, tt=tt: e.tensor_max(out=mcols[:, tt:tt + 1], in0=small[:, 2:3], in1=small[:, 3:4]),
                                     waits=[t_a0, t_a1])
                        t_ac = em.op("v", lambda e, tt=tt: e.tensor_mul(out=small[:, 4:5], in0=acols[:, tt:tt + 1], in1=small[:, 0:1]),
                                     waits=tl([t_tm0], free.get(("sc", tt))))
                        t_ac2 = em.op("v", lambda e, tt=tt: e.scalar_tensor_tensor(out=small[:, 4:5], in0=bcols[:, tt:tt + 1],
                                                                                   scalar=small[:, 1:2], in1=small[:, 4:5],
                                                                                   op0=ALU.mult, op1=ALU.add)
                                      if False else e.tensor_mul(out=small[:, 5:6], in0=bcols[:, tt:tt + 1], in1=small[:, 1:2]),
                                      waits=[t_tm1, t_ac])
                        t_ac3 = em.op("v", lambda e: e.tensor_add(out=small[:, 4:5], in0=small[:, 4:5], in1=small[:, 5:6]), waits=[t_ac2])
                        t_wa = em.op("v", lambda e, tt=tt: e.tensor_copy(out=acols[:, tt:tt + 1], in_=small[:, 4:5]), waits=[t_ac3])
                        t_wb = em.op("v", lambda e, tt=tt: e.tensor_scalar(out=bcols[:, tt:tt + 1], in0=small[:, 4:5], scalar1=-1.0,
                                                                           scalar2=1.0, op0=ALU.mult, op1=ALU.add), waits=[t_ac3])
                        free["rt"] = [t_mv, t_as0, t_as1]
                        free[("rout", tt)] = [t_mv, t_wa, t_wb]
                        t_cs = em.op("t", lambda e, tt=tt: e.matmul(psD[0:1, 2 * E:4 * E], s_onesc[:], ohst[:, tt * 2 * E:(tt + 1) * 2 * E],
                                                                    start=True, stop=True),
                                     waits=tl(tok_run))
                        t_run = em.op("v", lambda e: e.tensor_add(out=runsb[0:1, 0:2 * E], in0=runsb[0:1, 0:2 * E], in1=psD[0:1, 2 * E:4 * E]),
                                      waits=tl([t_cs], tok_run))
                        tok_run = [t_run]
                    tok_rout = tl(*[free[("rout", tt)] for tt in range(NT)])
                    t_trm = em.op("t", lambda e: e.transpose(psA[0:NT, 0:128], mcols[:], s_ident[:]), waits=tl(tok_rout, free.get("psab0")))
                    t_evm = em.op("v", lambda e: e.tensor_copy(out=mrows[:], in_=psA[0:NT, 0:128]), waits=[t_trm])
                    t_tra = em.op("t", lambda e: e.transpose(psB[0:NT, 0:128], acols[:], s_ident[:]), waits=tl(tok_rout, [t_trm]))
                    t_eva = em.op("v", lambda e: e.tensor_copy(out=arows[:], in_=psB[0:NT, 0:128]), waits=[t_tra])
                    t_trb = em.op("t", lambda e: e.transpose(psA[0:NT, 0:128], bcols[:], s_ident[:]), waits=tl(tok_rout, [t_evm]))
                    t_evb = em.op("v", lambda e: e.tensor_copy(out=brows[:], in_=psA[0:NT, 0:128]), waits=[t_trb])
                    free["psab0"] = t_evb
                    free["psab1"] = t_eva
                    free["psD"] = tok_run[0]
                    for tch in range(Lc // 512):
                        a0 = tch * 512
                        t_bb = []
                        for q in range(4):
                            ttq = tch * 4 + q
                            t_bb.append(em.op("t", lambda e, ttq=ttq, q=q: e.matmul(
                                psC[:, q * 128:(q + 1) * 128], s_csel16[:, ttq * 128:(ttq + 1) * 128], brows[:],
                                start=True, stop=True),
                                waits=tl([t_evb], tok_const, free.get("psC"))))
                        for ci in range(DC):
                            xb = xch3[ci % 2]
                            t_lh = em.dma("s", lambda e, ci=ci, a0=a0, xb=xb: e.dma_start(out=xb[:], in_=hdr[ci, :, a0:a0 + 512]),
                                          waits=tl(free.get(("xch", ci % 2))))
                            t_lx = em.dma("s", lambda e, ci=ci, a0=a0: e.dma_start(out=gsb[:], in_=xdr[ci, :, a0:a0 + 512]),
                                          waits=tl(free.get("gsb")))
                            t_br2 = bcast_row(stat[0:1, 2 * Lc + a0:2 * Lc + a0 + 512], 512, psD, tl(free.get("psD")))
                            t1 = em.op("v", lambda e, xb=xb: e.tensor_mul(out=sq_s[:, 0:512], in0=xb[:], in1=psD[:, 0:512]),
                                       waits=tl([t_lh, t_br2], free.get("sq")))
                            free["psD"] = t1
                            t2 = em.op("v", lambda e, ci=ci: e.tensor_scalar(out=sq_s[:, 0:512], in0=sq_s[:, 0:512],
                                                                             scalar1=s_g2[:, ci:ci + 1], scalar2=None, op0=ALU.mult), waits=[t1])
                            t3 = em.op("v", lambda e: e.tensor_mul(out=sq_s[:, 0:512], in0=sq_s[:, 0:512], in1=psC[:, 0:512]),
                                       waits=tl([t2], t_bb))
                            t4 = em.op("v", lambda e, xb=xb: e.tensor_add(out=gsb[:], in0=gsb[:], in1=xb[:]), waits=[t_lx, t_lh])
                            free[("xch", ci % 2)] = t4
                            t5 = em.op("v", lambda e: e.tensor_add(out=gsb[:], in0=gsb[:], in1=sq_s[:, 0:512]), waits=[t3, t4])
                            free["sq"] = t5
                            t_st = em.dma("a", lambda e, ci=ci, a0=a0: e.dma_start(out=xdr[ci, :, a0:a0 + 512], in_=gsb[:]), waits=[t5])
                            free["gsb"] = t_st
                            free["psC"] = tl(free.get("psC"), [t3]) if isinstance(free.get("psC"), list) else [t3]
                            free["psD"] = tl([t1])
                    t_z = [em.op("v", lambda e, ci=ci: e.memset(ysel[ci][:], 0.0)) for ci in range(DC)]
                    tok_ysel = {ci: [t_z[ci]] for ci in range(DC)}
                    for ex in range(E):
                        t_me = em.op("v", lambda e, ex=ex: e.tensor_scalar(out=merows[:], in0=mrows[:], scalar1=float(ex),
                                                                           scalar2=None, op0=ALU.is_equal),
                                     waits=tl([t_evm], free.get("merows")))
                        for th in range(c.NTH):
                            b0 = th * c.TH
                            for hc in range(c.HC):
                                t_l1_ = em.dma("s", lambda e, ex=ex, hc=hc: e.dma_start(
                                    out=wstg[0][:, 0:DC * 128].rearrange("p (kc n) -> p kc n", kc=DC), in_=W["fw1"][ex, :, hc * 128:(hc + 1) * 128].rearrange("(kc p) n -> p kc n", p=128)),
                                    waits=tl(free.get(("wstg", 0))))
                                t_c1_ = em.op("p", lambda e: e.tensor_copy(out=wbfg[0][:, 0:DC * 128], in_=wstg[0][:, 0:DC * 128]),
                                              waits=tl([t_l1_], free.get(("wbfg", 0))))
                                free[("wstg", 0)] = t_c1_
                                t_l2_ = em.dma("s", lambda e, ex=ex, hc=hc: e.dma_start(
                                    out=wstg[1][:, 0:DC * 128].rearrange("p (kc n) -> p kc n", kc=DC), in_=W["fw2"][ex, :, hc * 128:(hc + 1) * 128].rearrange("(kc p) n -> p kc n", p=128)),
                                    waits=tl(free.get(("wstg", 1))))
                                t_c2_ = em.op("p", lambda e: e.tensor_copy(out=wbfg[1][:, 0:DC * 128], in_=wstg[1][:, 0:DC * 128]),
                                              waits=tl([t_l2_], free.get(("wbfg", 1))))
                                free[("wstg", 1)] = t_c2_
                                for tc2 in range(c.TH // 512):
                                    a0 = b0 + tc2 * 512
                                    t_mm1 = []
                                    t_mm2 = []
                                    for kc in range(DC):
                                        t_mm1.append(em.op("t", lambda e, kc=kc, a0=a0: e.matmul(
                                            psA[:, 0:512], wbfg[0][:, kc * 128:(kc + 1) * 128], xfbf[kc][:, a0:a0 + 512],
                                            start=(kc == 0), stop=(kc == DC - 1)),
                                            waits=tl([t_c1_], tok_xf, free.get("psab0"))))
                                        t_mm2.append(em.op("t", lambda e, kc=kc, a0=a0: e.matmul(
                                            psB[:, 0:512], wbfg[1][:, kc * 128:(kc + 1) * 128], xfbf[kc][:, a0:a0 + 512],
                                            start=(kc == 0), stop=(kc == DC - 1)),
                                            waits=tl([t_c2_], tok_xf, free.get("psab1"))))
                                    t_ge = em.op("a", lambda e: e.activation(out=gsb[:], in_=psA[:, 0:512], func=AF.Identity),
                                                 waits=tl(t_mm1, free.get("gsb")))
                                    free["psab0"] = t_ge
                                    t_pr = em.op("v", lambda e: e.tensor_mul(out=msk[:], in0=psB[:, 0:512], in1=gsb[:]),
                                                 waits=tl(t_mm2, [t_ge], free.get("mskf")))
                                    free["psab1"] = t_pr
                                    free["gsb"] = t_pr
                                    t_gl = em.op("a", lambda e: e.activation(out=sq_s[:, 0:512], in_=msk[:], func=AF.Gelu),
                                                 waits=tl([t_pr], free.get("sq")))
                                    free["mskf"] = t_gl
                                    t_hg = em.op("v", lambda e, hc=hc, a0=a0, b0=b0: e.tensor_copy(out=hgbf[hc][:, a0 - b0:a0 - b0 + 512], in_=sq_s[:, 0:512]),
                                                 waits=[t_gl])
                                    free["sq"] = t_hg
                                    free[("hg", hc)] = t_hg
                                    free[("wbfg", 0)] = t_mm1[-1]
                                    free[("wbfg", 1)] = t_mm2[-1]
                            for oc in range(DC):
                                t_l3_ = em.dma("s", lambda e, ex=ex, oc=oc: e.dma_start(
                                    out=wstg[0][:, 0:c.HC * 128].rearrange("p (kc n) -> p kc n", kc=c.HC), in_=W["fw3"][ex, :, oc * 128:(oc + 1) * 128].rearrange("(kc p) n -> p kc n", p=128)),
                                    waits=tl(free.get(("wstg", 0))))
                                t_c3_ = em.op("p", lambda e: e.tensor_copy(out=wbfg[0][:, 0:c.HC * 128], in_=wstg[0][:, 0:c.HC * 128]),
                                              waits=tl([t_l3_], free.get(("wbfg", 0))))
                                free[("wstg", 0)] = t_c3_
                                for tc2 in range(c.TH // 512):
                                    a0 = b0 + tc2 * 512
                                    t_mm3 = []
                                    for kc in range(c.HC):
                                        t_mm3.append(em.op("t", lambda e, kc=kc, a0=a0, b0=b0: e.matmul(
                                            psA[:, 0:512], wbfg[0][:, kc * 128:(kc + 1) * 128], hgbf[kc][:, a0 - b0:a0 - b0 + 512],
                                            start=(kc == 0), stop=(kc == c.HC - 1)),
                                            waits=tl([t_c3_], [free.get(("hg", kc))], free.get("psab0"))))
                                    t_mb = []
                                    for q in range(4):
                                        ttq = a0 // 128 + q
                                        t_mb.append(em.op("t", lambda e, ttq=ttq, q=q: e.matmul(
                                            psC[:, q * 128:(q + 1) * 128], s_csel16[:, ttq * 128:(ttq + 1) * 128], merows[:],
                                            start=True, stop=True),
                                            waits=tl([t_me], tok_const, free.get("psC"))))
                                    t_mk = em.op("v", lambda e: e.tensor_copy(out=msk[:], in_=psC[:, 0:512]),
                                                 waits=tl(t_mb, free.get("mskf")))
                                    free["psC"] = t_mk
                                    t_sel = em.op("v", lambda e: e.tensor_mul(out=gsb[:], in0=psA[:, 0:512], in1=msk[:]),
                                                  waits=tl(t_mm3, [t_mk], free.get("gsb")))
                                    free["psab0"] = t_sel
                                    free[("wbfg", 0)] = t_mm3[-1]
                                    free["mskf"] = t_sel
                                    t_acc = em.op("v", lambda e, oc=oc, a0=a0: e.tensor_add(out=ysel[oc][:, a0:a0 + 512],
                                                                                            in0=ysel[oc][:, a0:a0 + 512], in1=gsb[:]),
                                                  waits=tl([t_sel], tok_ysel[oc]))
                                    tok_ysel[oc] = [t_acc]
                                    free["gsb"] = t_acc
                        free["merows"] = t_sel
                    for tch in range(Lc // 512):
                        a0 = tch * 512
                        t_ab = []
                        for q in range(4):
                            ttq = tch * 4 + q
                            t_ab.append(em.op("t", lambda e, ttq=ttq, q=q: e.matmul(
                                psC[:, q * 128:(q + 1) * 128], s_csel16[:, ttq * 128:(ttq + 1) * 128], arows[:],
                                start=True, stop=True),
                                waits=tl([t_eva], tok_const, free.get("psC"))))
                        for ci in range(DC):
                            t_lx = em.dma("s", lambda e, ci=ci, a0=a0: e.dma_start(out=gsb[:], in_=xdr[ci, :, a0:a0 + 512]),
                                          waits=tl(free.get("gsb")))
                            t1 = em.op("v", lambda e, ci=ci, a0=a0: e.tensor_mul(out=sq_s[:, 0:512], in0=ysel[ci][:, a0:a0 + 512], in1=psC[:, 0:512]),
                                       waits=tl(tok_ysel[ci], t_ab, free.get("sq")))
                            t2 = em.op("v", lambda e: e.tensor_add(out=gsb[:], in0=gsb[:], in1=sq_s[:, 0:512]), waits=[t_lx, t1])
                            free["sq"] = t2
                            t_st = em.dma("a", lambda e, ci=ci, a0=a0: e.dma_start(out=xdr[ci, :, a0:a0 + 512], in_=gsb[:]), waits=[t2])
                            free["gsb"] = t_st
                            free["psC"] = tl(free.get("psC"), [t1]) if isinstance(free.get("psC"), list) else [t1]
                    em.barrier()

        if "ma" in stages:
            mamba("a")
        for st in stages:
            if st.startswith("f0"):
                ffn(0, parts=tuple(st.split(":")[1].split(",")) if ":" in st else ("attn", "wo", "moe"))
        if "mb" in stages:
            mamba("b")
        if "f1" in stages:
            ffn(1)

        t_out = []
        tok_ps = {0: None, 1: None}
        prev_tr = None
        for tt in range(NT):
            evs = []
            for ci in range(DC):
                ps = [psA, psB][ci % 2]
                buf = ci % 2
                t_l = em.dma("s", lambda e, ci=ci, tt=tt, buf=buf: e.dma_start(out=t_s[:, buf * 128:(buf + 1) * 128], in_=xdr[ci, :, tt * 128:(tt + 1) * 128]),
                             waits=tl(tok_ps[ci % 2], [prev_tr] if prev_tr else None, t_out[-1:] if t_out else None))
                t_tr = em.op("t", lambda e, ps=ps, buf=buf: e.transpose(ps[:, 0:128], t_s[:, buf * 128:(buf + 1) * 128], s_ident[:]),
                             waits=tl([t_l], tok_ps[ci % 2]))
                prev_tr = t_tr
                t_ev = em.op("v", lambda e, ci=ci, ps=ps: e.tensor_copy(out=xtm[:, ci * 128:(ci + 1) * 128], in_=ps[:, 0:128]),
                             waits=[t_tr])
                tok_ps[ci % 2] = t_ev
                evs.append(t_ev)
            t_st = em.dma("s", lambda e, tt=tt: e.dma_start(out=x_out[tt * 128:(tt + 1) * 128, :], in_=xtm[:]), waits=tl(evs))
            t_out.append(t_st)
        em.op("s", lambda e: e.nop(), waits=tl(t_out))
        em.barrier()

        block = es.enter_context(nc.Block())
        em.finalize(block)
    return nc


def _rope_tables(Lc, off, HD=64):
    thetas = 10000.0 ** (-np.arange(0, HD, 2, dtype=np.float32) / HD)
    pos = np.arange(off, off + Lc, dtype=np.float32)
    ticks = np.outer(pos, thetas)
    cos = np.cos(ticks).astype(np.float32)
    sin = np.sin(ticks).astype(np.float32)
    cosf = np.concatenate([cos, cos], axis=1).T
    sinf = np.concatenate([sin, sin], axis=1).T
    return np.concatenate([cosf, cosf], axis=0), np.concatenate([sinf, sinf], axis=0)


def make_core_inputs(cfg, full, core):
    c = cfg
    b, h = core // 2, core % 2
    Lc = c.Lc
    x = np.ascontiguousarray(full["L0_attn_a_x"][b, h * Lc:(h + 1) * Lc, :]).astype(np.float32)
    def pmaj(v, inner):
        n = v.shape[0] // 128
        return np.ascontiguousarray(v.reshape(n, 128, inner).transpose(1, 0, 2).reshape(128, n * inner))
    if h == 1:
        xp3a = pmaj(full["L0_attn_a_x"][b, Lc - 3:Lc, :].T.copy(), 3)
    else:
        xp3a = np.zeros((128, c.DC * 3), np.float32)

    def pad_wx_a(Wx):
        out = np.zeros((c.DI, 128), np.float32)
        out[:, 0:c.DS] = Wx[:, c.DTR:c.DTR + c.DS]
        out[:, 32:32 + c.DS] = Wx[:, c.DTR + c.DS:c.DTR + 2 * c.DS]
        out[:, 64:64 + c.DTR] = Wx[:, 0:c.DTR]
        return out

    def pad_wx_b(Wx):
        out = np.zeros((c.DI, 128), np.float32)
        out[:, 0:1] = Wx[:, 0:1]
        out[:, 32:32 + c.DS] = Wx[:, 1:1 + c.DS]
        return out

    d = {}
    d["x"] = x
    d["xp3a"] = xp3a
    d["aWin"] = full["L0_attn_a_W_in"]
    d["aconvw"] = pmaj(full["L0_attn_a_conv_w"][:, 0, :], 4)
    d["aconvb"] = pmaj(full["L0_attn_a_conv_b"][:, None], 1)
    d["aWx"] = pad_wx_a(full["L0_attn_a_W_x"])
    d["aAlog"] = pmaj(full["L0_attn_a_A_log"], 16)
    d["aDp"] = pmaj(full["L0_attn_a_D"][:, None], 1)
    d["aWout"] = full["L0_attn_a_W_out"]
    d["aWdt"] = full["L0_attn_a_W_dt"]
    d["abdt"] = pmaj(full["L0_attn_a_b_dt"][:, None], 1)
    d["bWin"] = full["L0_attn_b_W_in"]
    d["bconvw"] = pmaj(full["L0_attn_b_conv_w"][:, 0, :], 4)
    d["bconvb"] = pmaj(full["L0_attn_b_conv_b"][:, None], 1)
    wxb = np.zeros((c.DI, 128), np.float32)
    wxb[:, 0:1] = full["L0_attn_b_W_x"][:, 0:1]
    wxb[:, 0:c.DS] = 0.0
    wxb[:, 0] = full["L0_attn_b_W_x"][:, 0]
    wxb[:, 0 + 0:0 + 0] = 0
    wxb = np.zeros((c.DI, 128), np.float32)
    wxb[:, 0:c.DS] = full["L0_attn_b_W_x"][:, 1:1 + c.DS]
    wxb[:, 32:32 + c.DS] = full["L0_attn_b_W_x"][:, 1 + c.DS:1 + 2 * c.DS]
    wxb[:, 64] = full["L0_attn_b_W_x"][:, 0]
    d["bWx"] = wxb
    d["bAlog"] = pmaj(full["L0_attn_b_A_log"], 16)
    d["bDp"] = pmaj(full["L0_attn_b_D_param"][:, None], 1)
    d["bWout"] = full["L0_attn_b_W_out"]
    d["fg1"] = pmaj(full["L0_ffn_g1"][:, None], 1)
    d["fg2"] = pmaj(full["L0_ffn_g2"][:, None], 1)
    d["fwq"] = full["L0_ffn_wq"]
    d["fwk"] = full["L0_ffn_wk"]
    d["fwv"] = full["L0_ffn_wv"]
    d["fwo"] = full["L0_ffn_wo"]
    d["frw"] = full["L0_ffn_router_w"]
    d["frb"] = np.tile(full["L0_ffn_router_b"][None, :], (128, 1))
    d["fw1"] = full["L0_ffn_w1"]
    d["fw2"] = full["L0_ffn_w2"]
    d["fw3"] = full["L0_ffn_w3"]
    import ml_dtypes
    cosf, sinf = _rope_tables(Lc, h * Lc)
    d["cos2"] = cosf.astype(ml_dtypes.bfloat16)
    d["sin2"] = sinf.astype(ml_dtypes.bfloat16)
    KBQ, KB = c.KBQ, c.KB
    cm = np.zeros((128, KBQ * KB), np.float32)
    for m in range(KBQ):
        q = np.arange(128)[:, None] + m * 128
        k = np.arange(KB)[None, :]
        cm[:, m * KB:(m + 1) * KB] = (k <= q).astype(np.float32)
    d["cmaskb"] = cm
    flags = np.zeros((128, 8), np.float32)
    flags[:, 0] = 1.0 if h == 1 else 0.0
    flags[:, 1] = 0.0 if h == 1 else NEG
    flags[:, 2] = 1.0 if (core % 2) == 1 else 0.0
    flags[:, 3] = 1.0 - flags[0, 2]
    d["cflag"] = flags
    sel = np.zeros((c.DS, c.DS * 128), np.float32)
    for s_ in range(c.DS):
        sel[s_, s_ * 128:(s_ + 1) * 128] = 1.0
    d["csel"] = sel
    sel16 = np.zeros((c.NT, c.NT * 128), np.float32)
    for t_ in range(c.NT):
        sel16[t_, t_ * 128:(t_ + 1) * 128] = 1.0
    d["csel16"] = sel16
    d["ctri"] = np.triu(np.ones((128, 128), np.float32))
    d["cioE"] = np.tile(np.arange(c.E, dtype=np.float32)[None, :], (128, 1))
    pred = np.zeros((8, 1), np.float32)
    pred[:core, 0] = 1.0
    d["cpred"] = pred
    d["cident"] = np.eye(128, dtype=np.float32)
    d["ones1"] = np.ones((1, 128), np.float32)
    d["onesc"] = np.ones((128, 1), np.float32)
    return {k: (np.ascontiguousarray(v) if k in ("cos2", "sin2") else np.ascontiguousarray(v, dtype=np.float32)) for k, v in d.items()}


def kernel(**inputs):
    from concourse.bass_utils import run_bass_kernel_spmd
    cfg = Cfg()
    nc = build(cfg)
    in_maps = [make_core_inputs(cfg, inputs, core) for core in range(8)]
    res = run_bass_kernel_spmd(nc, in_maps, core_ids=list(range(8)))
    B, L = 4, 4096
    out = np.zeros((B, L, cfg.D), np.float32)
    for core in range(8):
        b, h = core // 2, core % 2
        out[b, h * cfg.Lc:(h + 1) * cfg.Lc, :] = res.results[core]["y"]
    return out
